# version 11
# speedup vs baseline: 1.2463x; 1.0018x over previous
import numpy as np
import concourse.bass as bass
import concourse.mybir as mybir
from concourse.bass_utils import run_bass_kernel_spmd
from concourse.library_config import mlp as _LIB_MLP
from concourse.replica_groups import maybe_share_collective_output_space

F32 = mybir.dt.float32
BF16 = mybir.dt.bfloat16
I16 = mybir.dt.int16
U32 = mybir.dt.uint32
AX = mybir.AxisListType
ALU = mybir.AluOpType
ACT = mybir.ActivationFunctionType

NCORE = 8
T = 2048
D = 1024
H = 16
HD = 64
PF = 128
FF = 4096
E = 8
CAP = 2048
TG = NCORE * T
NCH = T // 128
KD = D // 128
RG = [[0, 1, 2, 3, 4, 5, 6, 7]]
SLOTP = 2056
GRID_ROWS = E * SLOTP

_CACHE = {}


def build(upto=6):
    nc = bass.Bass("TRN2", target_bir_lowering=False, detect_race_conditions=False,
                   num_devices=NCORE)

    x_nat = nc.dram_tensor("x_nat", [T, D], F32, kind="ExternalInput").ap()
    xT_in = nc.dram_tensor("xT_in", [D, T], F32, kind="ExternalInput").ap()
    wqkvT = nc.dram_tensor("wqkvT", [D, 3 * D], F32, kind="ExternalInput").ap()
    proj_in = nc.dram_tensor("proj_in", [H, HD, PF], F32, kind="ExternalInput").ap()
    woutT_in = nc.dram_tensor("woutT_in", [D, D], F32, kind="ExternalInput").ap()
    wout_n = nc.dram_tensor("wout_n", [D, D], F32, kind="ExternalInput").ap()
    bout_in = nc.dram_tensor("bout_in", [1, D], F32, kind="ExternalInput").ap()
    wr_in = nc.dram_tensor("wr_in", [D, E], F32, kind="ExternalInput").ap()
    w1_in = nc.dram_tensor("w1_in", [D, FF], F32, kind="ExternalInput").ap()
    w2_in = nc.dram_tensor("w2_in", [FF, D], F32, kind="ExternalInput").ap()
    ident_in = nc.dram_tensor("ident_in", [128, 128], F32, kind="ExternalInput").ap()
    slt_in = nc.dram_tensor("slt_in", [128, 128], F32, kind="ExternalInput").ap()
    iota8_in = nc.dram_tensor("iota8_in", [128, E], F32, kind="ExternalInput").ap()
    giota1_in = nc.dram_tensor("giota1_in", [128, 128], F32, kind="ExternalInput").ap()
    giotat_in = nc.dram_tensor("giotat_in", [128, 16], F32, kind="ExternalInput").ap()
    ones_r = nc.dram_tensor("ones_r", [1, 128], F32, kind="ExternalInput").ap()
    ones_c = nc.dram_tensor("ones_c", [128, 1], F32, kind="ExternalInput").ap()
    myexp_in = nc.dram_tensor("myexp_in", [128, 1], F32, kind="ExternalInput").ap()
    coreid_in = nc.dram_tensor("coreid_in", [1, 1], U32, kind="ExternalInput").ap()

    out_h = nc.dram_tensor("out", [T, D], F32, kind="ExternalOutput").ap()

    z1hbm = nc.dram_tensor("z1hbm", [T, D], F32, kind="Internal").ap()
    l_src = nc.dram_tensor("l_src", [T, E], F32, kind="Internal").ap()
    x_src = nc.dram_tensor("x_src", [T, D], BF16, kind="Internal").ap()
    ye_src = nc.dram_tensor("ye_src", [CAP, D], F32, kind="Internal").ap()
    sp = maybe_share_collective_output_space("AllGather", RG)
    l_ag = nc.dram_tensor("l_ag", [TG, E], F32, kind="Internal", addr_space=sp).ap()
    x_ag = nc.dram_tensor("x_ag", [TG, D], BF16, kind="Internal", addr_space=sp).ap()
    ye_ag = nc.dram_tensor("ye_ag", [E * CAP, D], F32, kind="Internal",
                           addr_space=sp).ap()
    grid_fl = nc.dram_tensor("grid_fl", [GRID_ROWS * 64], F32, kind="Internal").ap()
    rs_out = nc.dram_tensor("rs_out", [GRID_ROWS // NCORE, 64], F32,
                            kind="Internal").ap()

    from contextlib import ExitStack
    with ExitStack() as _stk:
        ident = _stk.enter_context(nc.sbuf_tensor("ident", [128, 128], F32))
        identb = _stk.enter_context(nc.sbuf_tensor("identb", [128, 128], BF16))
        sltb = _stk.enter_context(nc.sbuf_tensor("sltb", [128, 128], BF16))
        iota8 = _stk.enter_context(nc.sbuf_tensor("iota8", [128, E], F32))
        giota1 = _stk.enter_context(nc.sbuf_tensor("giota1", [128, 128], F32))
        giotat = _stk.enter_context(nc.sbuf_tensor("giotat", [128, 16], F32))
        lfin_dbg = _stk.enter_context(nc.sbuf_tensor("lfin_dbg", [128, NCH, E], F32))
        v1_dbg = _stk.enter_context(nc.sbuf_tensor("v1_dbg", [128, D], F32))
        rs_dbg = _stk.enter_context(nc.sbuf_tensor("rs_dbg", [128, 16], F32))
        onesr = _stk.enter_context(nc.sbuf_tensor("onesr", [1, 128], F32))
        onesrb = _stk.enter_context(nc.sbuf_tensor("onesrb", [1, 128], BF16))
        onesc = _stk.enter_context(nc.sbuf_tensor("onesc", [128, 1], F32))
        myexp = _stk.enter_context(nc.sbuf_tensor("myexp", [128, 1], F32))
        coreid = _stk.enter_context(nc.sbuf_tensor("coreid", [1, 1], U32))
        boutr = _stk.enter_context(nc.sbuf_tensor("boutr", [1, D], F32))
        wrf = _stk.enter_context(nc.sbuf_tensor("wrf", [128, KD, E], F32))
        wrb = _stk.enter_context(nc.sbuf_tensor("wrb", [128, KD, E], BF16))
        m_sb = _stk.enter_context(nc.sbuf_tensor("m_sb", [128, KD, E], BF16))
        bwr = _stk.enter_context(nc.sbuf_tensor("bwr", [1, E], BF16))
        s_sb = _stk.enter_context(nc.sbuf_tensor("s_sb", [128, E], F32))
        bb = _stk.enter_context(nc.sbuf_tensor("bb", [128, D], F32))
        projb = _stk.enter_context(nc.sbuf_tensor("projb", [128, H // 2, PF], BF16))
        rs_all = _stk.enter_context(nc.sbuf_tensor("rs_all", [128, NCH], F32))
        mrs_all = _stk.enter_context(nc.sbuf_tensor("mrs_all", [128, NCH], F32))
        attn_nat = _stk.enter_context(nc.sbuf_tensor("attn_nat", [128, NCH, D], BF16))
        t8b = _stk.enter_context(nc.sbuf_tensor("t8b", [128, E], BF16))
        b_sb = _stk.enter_context(nc.sbuf_tensor("b_sb", [128, E], F32))
        m16 = _stk.enter_context(nc.sbuf_tensor("m16", [16, TG // 16], I16))
        g16 = _stk.enter_context(nc.sbuf_tensor("g16", [16, TG // 16], I16))
        xidx = _stk.enter_context(nc.sbuf_tensor("xidx", [128, CAP // 16], I16))
        cidx = _stk.enter_context(nc.sbuf_tensor("cidx", [128, T // 16], I16))
        nfound = _stk.enter_context(nc.sbuf_tensor("nfound", [1, 1], U32))
        scalec = _stk.enter_context(nc.sbuf_tensor("scalec", [128, 16], F32))
        epsA = _stk.enter_context(nc.sbuf_tensor("epsA", [128, 1], F32))
        s_ldw = _stk.enter_context(nc.semaphore("s_ldw"))
        s_st = _stk.enter_context(nc.semaphore("s_st"))
        s_ldg = _stk.enter_context(nc.semaphore("s_ldg"))
        s_dg = _stk.enter_context(nc.semaphore("s_dg"))
        s_te = _stk.enter_context(nc.semaphore("s_te"))
        s_ve = _stk.enter_context(nc.semaphore("s_ve"))
        s_se = _stk.enter_context(nc.semaphore("s_se"))
        s_cc = _stk.enter_context(nc.semaphore("s_cc"))
        s_out = _stk.enter_context(nc.semaphore("s_out"))
        s_at = _stk.enter_context(nc.semaphore("s_at"))
        lg_ps = _stk.enter_context(nc.psum_tensor("lg_ps", [128, NCH, E], F32))
        block = _stk.enter_context(nc.Block())
        n = dict(ldw=0, st=0, ldg=0, dg=0, te=0, ve=0, se=0, cc=0, out=0)

        @block.sync
        def _(sync):
            for dst, src in ((ident, ident_in), (iota8, iota8_in),
                             (giota1, giota1_in), (giotat, giotat_in),
                             (onesr, ones_r),
                             (onesc, ones_c), (myexp, myexp_in),
                             (coreid, coreid_in), (boutr, bout_in)):
                sync.dma_start(out=dst[:], in_=src).then_inc(s_ldw, 16)
            sync.dma_start(out=wrf[:],
                           in_=wr_in.rearrange("(k p) e -> p k e", p=128)
                           ).then_inc(s_ldw, 16)
        n["ldw"] += 10 * 16
        P0_LDW = n["ldw"]

        @block.gpsimd
        def _(gpsimd):
            gpsimd.dma_start(out=identb[:], in_=ident_in).then_inc(s_ldg, 16)
            gpsimd.dma_start(out=sltb[:], in_=slt_in).then_inc(s_ldg, 16)
            gpsimd.dma_start(out=onesrb[:], in_=ones_r).then_inc(s_ldg, 16)
            gpsimd.dma_start(out=wrb[:],
                             in_=wr_in.rearrange("(k p) e -> p k e", p=128)
                             ).then_inc(s_ldg, 16)
            for h in range(H):
                gpsimd.dma_start(out=projb[64 * (h % 2):64 * (h % 2) + 64, h // 2, :],
                                 in_=proj_in[h]).then_inc(s_ldg, 16)
        n["ldg"] += (4 + H) * 16
        P0_LDG = n["ldg"]

        with (
            nc.sbuf_tensor("woutf", [128, KD, D], F32) as woutf,
            nc.sbuf_tensor("boutk", [128, KD], F32) as boutk,
            nc.sbuf_tensor("srow", [1, E], F32) as srow,
            nc.sbuf_tensor("bwrf", [1, E], F32) as bwrf,
            nc.psum_tensor("pp_a", [128, 512], F32) as pp_a,
            nc.psum_tensor("pp_b", [128, E], F32) as pp_b,
            nc.psum_tensor("pp_c", [1, E], F32) as pp_c,
        ):
            @block.sync
            def _(sync):
                sync.dma_start(out=woutf[:],
                               in_=wout_n.rearrange("(k p) d -> p k d", p=128)
                               ).then_inc(s_ldw, 16)
                sync.dma_start(out=boutk[:],
                               in_=bout_in.rearrange("o (k p) -> p (o k)", p=128),
                               allow_slow_non_contiguous=True
                               ).then_inc(s_ldw, 16)
            n["ldw"] += 2 * 16
            P0B_LDW = n["ldw"]

            te0, ve0 = n["te"], n["ve"]

            @block.tensor
            def _(tensor):
                tensor.wait_ge(s_ldw, P0B_LDW)
                for dk in range(KD):
                    if dk >= 1:
                        tensor.wait_ge(s_ve, ve0 + dk)
                    mm = None
                    for k in range(KD):
                        mm = tensor.matmul(pp_b[:],
                                           woutf[:, k, 128 * dk:128 * dk + 128],
                                           wrf[:, k, :], start=(k == 0),
                                           stop=(k == KD - 1))
                    mm.then_inc(s_te, 1)
                mm = None
                for k in range(KD):
                    mm = tensor.matmul(pp_c[:], onesc[:], wrf[:, k, :],
                                       start=(k == 0), stop=(k == KD - 1))
                mm.then_inc(s_te, 1)
                tensor.wait_ge(s_ve, ve0 + KD + 1)
                mm = None
                for k in range(KD):
                    mm = tensor.matmul(pp_c[:], boutk[:, k:k + 1], wrf[:, k, :],
                                       start=(k == 0), stop=(k == KD - 1))
                mm.then_inc(s_te, 1)
                tensor.matmul(pp_a[:], onesr[:], boutr[:, 0:512], start=True,
                              stop=True)
                tensor.matmul(pp_b[:], onesr[:], srow[:], start=True,
                              stop=True).then_inc(s_te, 1)
                tensor.wait_ge(s_ve, ve0 + KD + 4)
                tensor.matmul(pp_a[:], onesr[:], boutr[:, 512:1024], start=True,
                              stop=True).then_inc(s_te, 1)
            n["te"] += KD + 4

            @block.vector
            def _(vector):
                vector.wait_ge(s_ldw, P0_LDW)
                vector.tensor_scalar_mul(epsA[:], onesc[:], 1e-5)
                for dk in range(KD):
                    vector.wait_ge(s_te, te0 + dk + 1)
                    vector.tensor_copy(m_sb[:, dk, :], pp_b[:]).then_inc(s_ve, 1)
                vector.wait_ge(s_te, te0 + KD + 1)
                vector.tensor_copy(srow[:], pp_c[:]).then_inc(s_ve, 1)
                vector.wait_ge(s_te, te0 + KD + 2)
                vector.tensor_copy(bwrf[:], pp_c[:]).then_inc(s_ve, 1)
                vector.wait_ge(s_te, te0 + KD + 3)
                vector.tensor_copy(bb[:, 0:512], pp_a[:]).then_inc(s_ve, 1)
                vector.tensor_copy(s_sb[:], pp_b[:]).then_inc(s_ve, 1)
                vector.wait_ge(s_te, te0 + KD + 4)
                vector.tensor_copy(bb[:, 512:1024], pp_a[:]).then_inc(s_ve, 1)
            n["ve"] += KD + 5

            @block.scalar
            def _(scalar):
                scalar.wait_ge(s_ve, ve0 + KD + 2)
                scalar.copy(bwr[:], bwrf[:]).then_inc(s_se, 1)
            n["se"] += 1
        P0_TE, P0_VE = n["te"], n["ve"]

        with (
            nc.sbuf_tensor("qT", [128, KD, T], BF16) as qT,
            nc.sbuf_tensor("kT", [128, KD, T], BF16) as kT,
            nc.sbuf_tensor("v_ext", [128, NCH, H, HD + 1], BF16) as v_ext,
        ):
            with (
                nc.sbuf_tensor("xTb", [128, KD, T], BF16) as xTb,
                nc.sbuf_tensor("wqp", [128, KD, D], BF16) as wqp,
                nc.psum_tensor("qk_ps", [128, 2, 512], F32) as qk_ps,
            ):
                @block.gpsimd
                def _(gpsimd):
                    gpsimd.wait_ge(s_te, P0_TE)
                    for k in range(KD):
                        gpsimd.dma_start(out=xTb[:, k, :],
                                         in_=xT_in[128 * k:128 * k + 128, :]
                                         ).then_inc(s_ldg, 16)
                n["ldg"] += KD * 16
                P1_XT = n["ldg"]

                def emit_part(part, kind):
                    te0, ve0 = n["te"], n["ve"]
                    te_before = n["te"]

                    @block.gpsimd
                    def _(gpsimd):
                        if part > 0:
                            gpsimd.wait_ge(s_te, te_before)
                        for k in range(KD):
                            gpsimd.dma_start(
                                out=wqp[:, k, :],
                                in_=wqkvT[128 * k:128 * k + 128,
                                          part * D:(part + 1) * D]
                            ).then_inc(s_ldg, 16)
                    n["ldg"] += KD * 16
                    wready = n["ldg"]

                    if kind == "v":
                        @block.vector
                        def _(vector):
                            vector.memset(v_ext[:, :, :, HD:HD + 1], 1.0
                                          ).then_inc(s_ve, 1)
                        n["ve"] += 1
                        ve0 = n["ve"]

                    @block.tensor
                    def _(tensor):
                        if part == 0:
                            tensor.wait_ge(s_ve, P0_VE)
                        tensor.wait_ge(s_ldg, max(wready, P1_XT))
                        i = 0
                        for a in range(KD if kind != "v" else NCH):
                            for b_ in range(4 if kind != "v" else 2):
                                if i >= 2:
                                    tensor.wait_ge(s_ve, ve0 + i - 1)
                                ps = qk_ps[:, i % 2, :]
                                mm = None
                                for k in range(KD):
                                    if kind != "v":
                                        mm = tensor.matmul(
                                            ps, wqp[:, k, 128 * a:128 * a + 128],
                                            xTb[:, k, 512 * b_:512 * b_ + 512],
                                            start=(k == 0), stop=(k == KD - 1))
                                    else:
                                        mm = tensor.matmul(
                                            ps, xTb[:, k, 128 * a:128 * a + 128],
                                            wqp[:, k, 512 * b_:512 * b_ + 512],
                                            start=(k == 0), stop=(k == KD - 1))
                                mm.then_inc(s_te, 1)
                                i += 1

                    @block.vector
                    def _(vector):
                        i = 0
                        for a in range(KD if kind != "v" else NCH):
                            for b_ in range(4 if kind != "v" else 2):
                                vector.wait_ge(s_te, te0 + i + 1)
                                if kind == "q":
                                    vector.tensor_copy(
                                        qT[:, a, 512 * b_:512 * b_ + 512],
                                        qk_ps[:, i % 2, :]).then_inc(s_ve, 1)
                                elif kind == "k":
                                    vector.tensor_copy(
                                        kT[:, a, 512 * b_:512 * b_ + 512],
                                        qk_ps[:, i % 2, :]).then_inc(s_ve, 1)
                                else:
                                    vector.tensor_copy(
                                        v_ext[:, a, 8 * b_:8 * b_ + 8, 0:HD],
                                        qk_ps[:, i % 2, :].rearrange(
                                            "p (h f) -> p h f", h=8)
                                    ).then_inc(s_ve, 1)
                                i += 1
                    n["te"] += 32
                    n["ve"] += 32

                emit_part(0, "q")
                emit_part(1, "k")
                emit_part(2, "v")

                @block.tensor
                def _(tensor):
                    mm = None
                    for j in range(NCH):
                        for k in range(KD):
                            mm = tensor.matmul(lg_ps[:, j, :],
                                               xTb[:, k, 128 * j:128 * j + 128],
                                               wrb[:, k, :],
                                               start=(j == 0 and k == 0),
                                               stop=False)
                    mm.then_inc(s_te, 1)
                n["te"] += 1
            P1_TE, P1_VE = n["te"], n["ve"]

            if upto <= 1:
                _nve = n["ve"]; _nldg = n["ldg"]; _ndg = n["dg"]; _nte = n["te"]
                @block.scalar
                def _(scalar):
                    scalar.wait_ge(s_ve, _nve)
                    scalar.wait_ge(s_te, _nte)
                    scalar.dma_start(out=out_h[0:128, :], in_=bb[:]).then_inc(s_out, 16)
                    scalar.wait_ge(s_out, 16)
                @block.gpsimd
                def _(gpsimd):
                    gpsimd.wait_ge(s_ldg, _nldg)
                    gpsimd.wait_ge(s_dg, _ndg)
                mybir.codegen_inst_isa_subclasses(nc)
                return nc

            with (
                nc.sbuf_tensor("kp_h", [128, NCH, PF], BF16) as kp_h,
                nc.sbuf_tensor("qp_h", [128, 4, 512], BF16) as qp_h,
                nc.sbuf_tensor("kv_h", [128, HD + 1], BF16) as kv_h,
                nc.sbuf_tensor("el_u", [128, 2, 512], F32) as el_u,
                nc.sbuf_tensor("el_e", [128, 2, 512], F32) as el_e,
                nc.sbuf_tensor("araw", [128, 8, HD], F32) as araw,
                nc.sbuf_tensor("zb8b", [128, 8], F32) as zb8b,
                nc.sbuf_tensor("zb8c", [128, 8], F32) as zb8c,
                nc.sbuf_tensor("zb8d", [128, 8], F32) as zb8d,
                nc.sbuf_tensor("zb8e", [128, 8], F32) as zb8e,
                nc.psum_tensor("kp_ps", [128, 2, 4, 128], F32) as kp_ps,
                nc.psum_tensor("qp_ps", [128, 2, 512], F32) as qp_ps,
                nc.psum_tensor("pz", [128, 512], F32) as pz,
                nc.psum_tensor("at_ps", [128, 2, 8, HD], F32) as at_ps,
            ):
                def emit_attn(tensor, hprev, vb, sb):
                    for hf in range(2):
                        if hprev >= 1:
                            tensor.wait_ge(s_se, sb + 9 + 3 * hf)
                            tensor.wait_ge(s_ve, vb + 18 + 3 * hf)
                        else:
                            tensor.wait_ge(s_ve, vb + 17)
                        mm = None
                        for jj in range(8):
                            j = 8 * hf + jj
                            mm = tensor.matmul(
                                at_ps[:, hf, jj, :],
                                qp_h[:, j // 4, 128 * (j % 4):128 * (j % 4) + 128],
                                kv_h[:, 0:HD], start=True, stop=True)
                        for jj in range(8):
                            j = 8 * hf + jj
                            mm = tensor.matmul(
                                pz[:, 128 + 8 * hf + jj:129 + 8 * hf + jj],
                                qp_h[:, j // 4, 128 * (j % 4):128 * (j % 4) + 128],
                                kv_h[:, HD:HD + 1], start=True, stop=True)
                        mm.then_inc(s_at, 1)

                def emit_tail_s(scalar, hprev, ve0, se0, af0):
                    for hf in range(2):
                        scalar.wait_ge(s_at, 2 * hprev + 1 + hf)
                        scalar.wait_ge(s_ve, ve0 + 3 if hf == 1 else af0)
                        scalar.copy(araw[:], at_ps[:, hf, :, :]
                                    ).then_inc(s_se, 1)
                        scalar.wait_ge(s_ve, ve0 + 1 + 3 * hf)
                        scalar.copy(zb8c[:], zb8b[:]).then_inc(s_se, 1)
                        scalar.wait_ge(s_ve, ve0 + 2 + 3 * hf)
                        scalar.copy(zb8e[:], zb8d[:]).then_inc(s_se, 1)

                def emit_tail_v(vector, hprev, se0, sbase):
                    for hf in range(2):
                        vector.wait_ge(s_at, 2 * hprev + 1 + hf)
                        if hf == 0:
                            vector.wait_ge(s_se, max(0, sbase - 1))
                        vector.tensor_scalar_add(
                            zb8b[:], pz[:, 128 + 8 * hf:136 + 8 * hf], 1e-8
                            ).then_inc(s_ve, 1)
                        vector.wait_ge(s_se, se0 + 2 + 3 * hf)
                        vector.reciprocal(zb8d[:], zb8c[:]).then_inc(s_ve, 1)
                        vector.wait_ge(s_se, se0 + 3 + 3 * hf)
                        vector.tensor_tensor(
                            attn_nat[:, 8 * hf:8 * hf + 8,
                                     HD * hprev:HD * hprev + HD],
                            araw[:, :, :],
                            zb8e[:].unsqueeze(2).broadcast_to([128, 8, HD]),
                            ALU.mult).then_inc(s_ve, 1)

                VB, SB = [], []
                for h in range(H):
                    pb = 64 * (h % 2)
                    cb = h // 2
                    te0, ve0, se0 = n["te"], n["ve"], n["se"]
                    VB.append(ve0)
                    SB.append(se0)

                    @block.tensor
                    def _(tensor, h=h, pb=pb, cb=cb, te0=te0, ve0=ve0, se0=se0):
                        if h == 0:
                            tensor.wait_ge(s_ldg, P0_LDG)
                            tensor.wait_ge(s_ve, P1_VE)
                        for g in range(4):
                            if g >= 2:
                                tensor.wait_ge(s_ve, ve0 + 2 * g - 2)
                            elif h >= 1:
                                tensor.wait_ge(s_ve, VB[h - 1] + 6 + 2 * g)
                            mm = None
                            for jj in range(4):
                                j = 4 * g + jj
                                mm = tensor.matmul(
                                    kp_ps[:, g % 2, jj, :],
                                    kT[pb:pb + 64, cb, 128 * j:128 * j + 128],
                                    projb[pb:pb + 64, cb, :],
                                    start=True, stop=True)
                            mm.then_inc(s_te, 1)
                        if h >= 1:
                            emit_attn(tensor, h - 1, VB[h - 1], SB[h - 1])
                        for tc in range(4):
                            if tc >= 2:
                                tensor.wait_ge(s_ve, ve0 + 6 + 2 * tc)
                            tensor.matmul(qp_ps[:, tc % 2, :],
                                          projb[pb:pb + 64, cb, :],
                                          qT[pb:pb + 64, cb, 512 * tc:512 * tc + 512],
                                          start=True, stop=True).then_inc(s_te, 1)
                        tensor.wait_ge(s_ve, ve0 + 8)
                        mm = None
                        for j in range(NCH):
                            mm = tensor.matmul(pz[:, 0:HD + 1], kp_h[:, j, :],
                                               v_ext[:, j, h, :],
                                               start=(j == 0), stop=(j == NCH - 1))
                        mm.then_inc(s_te, 1)

                    @block.scalar
                    def _(scalar, h=h, ve0=ve0, se0=se0):
                        for g in range(4):
                            scalar.wait_ge(s_ve, ve0 + 2 * g + 1)
                            scalar.activation(el_e[:, g % 2, :], el_u[:, g % 2, :],
                                              ACT.Exp, bias=0.0, scale=1.0
                                              ).then_inc(s_se, 1)
                        for tc in range(4):
                            scalar.wait_ge(s_ve, ve0 + 9 + 2 * tc)
                            scalar.activation(el_e[:, tc % 2, :], el_u[:, tc % 2, :],
                                              ACT.Exp, bias=0.0, scale=1.0
                                              ).then_inc(s_se, 1)
                        if h >= 1:
                            emit_tail_s(scalar, h - 1, ve0 + 17, se0 + 8, ve0)

                    @block.vector
                    def _(vector, h=h, te0=te0, ve0=ve0, se0=se0):
                        for g in range(4):
                            vector.wait_ge(s_te, te0 + g + 1)
                            if g >= 2:
                                vector.wait_ge(s_se, se0 + g - 1)
                            vector.tensor_scalar_min(
                                el_u[:, g % 2, :],
                                kp_ps[:, g % 2, :, :].rearrange(
                                    "p a b -> p (a b)"), 0.0
                                ).then_inc(s_ve, 1)
                            vector.wait_ge(s_se, se0 + g + 1)
                            vector.scalar_tensor_tensor(
                                kp_h.rearrange("p a b -> p (a b)")[
                                    :, 512 * g:512 * g + 512],
                                kp_ps[:, g % 2, :, :].rearrange(
                                    "p a b -> p (a b)"), 0.0,
                                el_e[:, g % 2, :],
                                op0=ALU.max, op1=ALU.add).then_inc(s_ve, 1)
                        for tc in range(4):
                            vector.wait_ge(s_te, te0 + 5 + tc)
                            vector.wait_ge(s_se, se0 + 3 + tc)
                            vector.tensor_scalar_min(el_u[:, tc % 2, :],
                                                     qp_ps[:, tc % 2, :],
                                                     0.0).then_inc(s_ve, 1)
                            vector.wait_ge(s_se, se0 + 5 + tc)
                            vector.scalar_tensor_tensor(
                                qp_h[:, tc, :], qp_ps[:, tc % 2, :], 0.0,
                                el_e[:, tc % 2, :],
                                op0=ALU.max, op1=ALU.add).then_inc(s_ve, 1)
                        vector.wait_ge(s_te, te0 + 9)
                        vector.tensor_copy(kv_h[:], pz[:, 0:HD + 1]).then_inc(s_ve, 1)
                        if h >= 1:
                            emit_tail_v(vector, h - 1, se0 + 8, se0)

                    n["te"] += 9
                    n["se"] += 8 + (6 if h >= 1 else 0)
                    n["ve"] += 17 + (6 if h >= 1 else 0)

                te0, ve0, se0 = n["te"], n["ve"], n["se"]

                @block.tensor
                def _(tensor):
                    emit_attn(tensor, H - 1, VB[H - 1], SB[H - 1])

                @block.scalar
                def _(scalar):
                    emit_tail_s(scalar, H - 1, ve0, se0, ve0)

                @block.vector
                def _(vector):
                    emit_tail_v(vector, H - 1, se0, se0)

                n["se"] += 6
                n["ve"] += 6
                if upto <= 2:
                    _nve2 = n["ve"]; _nse2 = n["se"]; _dbase = n["ldg"]
                    @block.gpsimd
                    def _(gpsimd):
                        gpsimd.wait_ge(s_ve, _nve2)
                        gpsimd.wait_ge(s_se, _nse2)
                        gpsimd.dma_start(out=out_h[768:896, 0:65], in_=kv_h[:]).then_inc(s_ldg, 16)
                        gpsimd.dma_start(out=out_h[896:1024, :],
                                         in_=qp_h.rearrange("p a b -> p (a b)")[:, 0:1024]).then_inc(s_ldg, 16)
                        gpsimd.dma_start(out=out_h[1024:1152, :],
                                         in_=kp_h.rearrange("p a b -> p (a b)")[:, 0:1024]).then_inc(s_ldg, 16)
                        gpsimd.dma_start(out=out_h[1152:1280, 0:8], in_=zb8d[:]).then_inc(s_ldg, 16)
                        gpsimd.dma_start(out=out_h[1280:1408, 0:512],
                                         in_=el_e[:, 1, :]).then_inc(s_ldg, 16)
                        gpsimd.wait_ge(s_ldg, _dbase + 80)
                    n["ldg"] += 80
            P2_TE, P2_VE, P2_SE = n["te"], n["ve"], n["se"]

            if upto <= 2:
                _nve = n["ve"]; _nldg = n["ldg"]; _ndg = n["dg"]; _nte = n["te"]
                _nse = n["se"]
                @block.scalar
                def _(scalar):
                    scalar.wait_ge(s_ve, _nve)
                    scalar.wait_ge(s_te, _nte)
                    scalar.dma_start(out=out_h[0:128, :], in_=bb[:]).then_inc(s_out, 16)
                    scalar.wait_ge(s_out, 16)
                @block.gpsimd
                def _(gpsimd):
                    gpsimd.wait_ge(s_ldg, _nldg)
                    gpsimd.wait_ge(s_dg, _ndg)
                    gpsimd.wait_ge(s_ve, _nve)
                    gpsimd.wait_ge(s_se, _nse)
                    gpsimd.dma_start(out=out_h[128:256, :], in_=attn_nat[:, 0, :]).then_inc(s_ldg, 16)
                    gpsimd.dma_start(out=out_h[256:384, :], in_=attn_nat[:, 8, :]).then_inc(s_ldg, 16)
                    gpsimd.dma_start(out=out_h[384:512, :], in_=qT[:, 0, 0:1024]).then_inc(s_ldg, 16)
                    gpsimd.dma_start(out=out_h[512:640, :], in_=kT[:, 0, 0:1024]).then_inc(s_ldg, 16)
                    gpsimd.dma_start(out=out_h[640:768, :],
                                     in_=v_ext.rearrange("p a b c -> p (a b c)")[:, 0:1024]).then_inc(s_ldg, 16)
                    gpsimd.wait_ge(s_ldg, _nldg + 80)
                mybir.codegen_inst_isa_subclasses(nc)
                return nc

        with (
            nc.sbuf_tensor("w1h", [128, KD, FF // 2], BF16) as w1h,
            nc.sbuf_tensor("w2h", [128, 16, D], BF16) as w2h,
        ):
            with (
                nc.sbuf_tensor("attnT", [128, KD, T], BF16) as attnT,
                nc.sbuf_tensor("woutb", [128, KD, D], BF16) as woutb,
                nc.sbuf_tensor("xbuf", [128, 2, D], F32) as xbuf,
                nc.sbuf_tensor("v1t", [128, 2, D], F32) as v1t,
                nc.sbuf_tensor("z1t", [128, 2, D], F32) as z1t,
                nc.sbuf_tensor("z1bt", [128, 2, D], BF16) as z1bt,
                nc.sbuf_tensor("s1m", [128, 1], F32) as s1m,
                nc.sbuf_tensor("t_mu", [128, 1], F32) as t_mu,
                nc.sbuf_tensor("t_a", [128, 1], F32) as t_a,
                nc.sbuf_tensor("v_b", [128, 1], F32) as v_b,
                nc.sbuf_tensor("sdt", [128, 1], F32) as sdt,
                nc.sbuf_tensor("rsig", [128, 1], F32) as rsig,
                nc.sbuf_tensor("rsig2", [128, 1], F32) as rsig2,
                nc.sbuf_tensor("ln_scr", [128, D], F32) as ln_scr,
                nc.sbuf_tensor("lfin", [128, NCH, E], F32) as lfin,
            ):
              @block.gpsimd
              def _(gpsimd):
                  gpsimd.wait_ge(s_te, P2_TE)
                  gpsimd.wait_ge(s_at, 2 * H)
                  for k in range(KD):
                      gpsimd.dma_start(out=woutb[:, k, :],
                                       in_=woutT_in[128 * k:128 * k + 128, :]
                                       ).then_inc(s_ldg, 16)
                  for k in range(KD):
                      gpsimd.dma_start(out=w1h[:, k, :],
                                       in_=w1_in[128 * k:128 * k + 128, 0:FF // 2]
                                       ).then_inc(s_ldg, 16)
                  for m in range(16):
                      gpsimd.dma_start(out=w2h[:, m, :],
                                       in_=w2_in[128 * m:128 * m + 128, :]
                                       ).then_inc(s_ldg, 16)
              n["ldg"] += (KD + KD + 16) * 16
              P3_WOUT = n["ldg"] - (KD + 16) * 16
              P3_W12 = n["ldg"]

              with (
                  nc.psum_tensor("tr_ps", [128, 4, 1024], BF16) as tr_ps,
                  nc.psum_tensor("y1_ps", [128, 2, 512], F32) as y1_ps,
              ):
                  te0, ve0, se0 = n["te"], n["ve"], n["se"]
                  ldw0, st0 = n["ldw"], n["st"]
                  TE_C, VE_C, SE_C = 9, 17, 4

                  @block.sync
                  def _(sync):
                      sync.wait_ge(s_te, P2_TE)
                      sync.wait_ge(s_at, 2 * H)
                      sync.wait_ge(s_ve, P2_VE)
                      sync.wait_ge(s_se, P2_SE)
                      for j in range(NCH):
                          if j >= 2:
                              sync.wait_ge(s_ve, ve0 + (j - 2) * VE_C + 11)
                          sync.dma_start(out=xbuf[:, j % 2, :],
                                         in_=x_nat[128 * j:128 * j + 128, :]
                                         ).then_inc(s_ldw, 16)
                  n["ldw"] += NCH * 16

                  @block.tensor
                  def _(tensor):
                      tensor.wait_ge(s_se, P2_SE)
                      tensor.wait_ge(s_ve, P2_VE)
                      tensor.wait_ge(s_ldg, P3_WOUT)
                      for j in range(NCH):
                          for w in range(KD):
                              i = j * KD + w
                              if i >= 4:
                                  tensor.wait_ge(s_ve, ve0 + (i - 4) // KD * VE_C
                                                 + (i - 4) % KD + 1)
                              tensor.transpose(tr_ps[:, i % 4, 0:128],
                                               attn_nat[:, j, 128 * w:128 * w + 128],
                                               identb[:]).then_inc(s_te, 1)
                          tensor.wait_ge(s_ve, ve0 + j * VE_C + KD)
                          if j >= 1:
                              tensor.wait_ge(s_ve, ve0 + (j - 1) * VE_C + KD + 2)
                          mm = None
                          for dh in range(2):
                              for k in range(KD):
                                  mm = tensor.matmul(
                                      y1_ps[:, dh, :],
                                      attnT[:, k, 128 * j:128 * j + 128],
                                      woutb[:, k, 512 * dh:512 * dh + 512],
                                      start=(k == 0), stop=(k == KD - 1))
                          for k in range(KD):
                              mm = tensor.matmul(lg_ps[:, j, :],
                                                 attnT[:, k, 128 * j:128 * j + 128],
                                                 m_sb[:, k, :], start=False,
                                                 stop=False)
                          mm = tensor.matmul(lg_ps[:, j, :], onesrb[:], bwr[:],
                                             start=False, stop=(j == NCH - 1))
                          mm.then_inc(s_te, 1)

                  @block.vector
                  def _(vector):
                      for j in range(NCH):
                          for w in range(KD):
                              i = j * KD + w
                              vector.wait_ge(s_te, te0 + j * (KD + 1) + w + 1)
                              vector.tensor_copy(attnT[:, w, 128 * j:128 * j + 128],
                                                 tr_ps[:, i % 4, 0:128]).then_inc(s_ve, 1)
                          vector.wait_ge(s_te, te0 + (j + 1) * (KD + 1))
                          vector.scalar_tensor_tensor(
                              v1t[:, j % 2, 0:512], y1_ps[:, 0, :], 1.0, bb[:, 0:512],
                              op0=ALU.mult, op1=ALU.add).then_inc(s_ve, 1)
                          vector.scalar_tensor_tensor(
                              v1t[:, j % 2, 512:1024], y1_ps[:, 1, :], 1.0,
                              bb[:, 512:1024], op0=ALU.mult, op1=ALU.add
                              ).then_inc(s_ve, 1)
                          vector.wait_ge(s_ldw, ldw0 + (j + 1) * 16)
                          vector.tensor_add(v1t[:, j % 2, :], v1t[:, j % 2, :],
                                            xbuf[:, j % 2, :]).then_inc(s_ve, 1)
                          vector.tensor_reduce(s1m[:].unsqueeze(2),
                                               v1t[:, j % 2, :].unsqueeze(1),
                                               axis=AX.X, op=ALU.add
                                               ).then_inc(s_ve, 1)
                          vector.wait_ge(s_se, se0 + j * SE_C + 2)
                          vector.scalar_tensor_tensor(v_b[:], t_mu[:], t_mu[:],
                                                      t_a[:], op0=ALU.mult,
                                                      op1=ALU.subtract
                                                      ).then_inc(s_ve, 1)
                          vector.wait_ge(s_se, se0 + j * SE_C + 3)
                          vector.reciprocal(rsig[:], sdt[:]).then_inc(s_ve, 1)
                          vector.wait_ge(s_se, se0 + j * SE_C + 4)
                          if j >= 2:
                              vector.wait_ge(s_st, st0 + (j - 1) * 32)
                          vector.tensor_scalar(z1t[:, j % 2, :], v1t[:, j % 2, :],
                                               t_mu[:], rsig2[:],
                                               op0=ALU.subtract, op1=ALU.mult
                                               ).then_inc(s_ve, 1)
                          vector.tensor_copy(z1bt[:, j % 2, :], z1t[:, j % 2, :]
                                             ).then_inc(s_ve, 1)
                          vector.tensor_copy(rs_all[:, j:j + 1], rsig2[:])
                          vector.tensor_mul(mrs_all[:, j:j + 1], t_mu[:], rsig2[:]
                                            ).then_inc(s_ve, 1)
                      vector.tensor_copy(xbuf[:, 0, :], v1t[:, 0, :])
                      vector.wait_ge(s_te, te0 + NCH * (KD + 1))
                      vector.tensor_tensor(
                          lfin[:], lg_ps[:],
                          rs_all[:].unsqueeze(2).broadcast_to([128, NCH, E]),
                          ALU.mult).then_inc(s_ve, 1)
                      vector.tensor_tensor(
                          v1t[:, 0, 0:128].rearrange("p (j e) -> p j e", e=E),
                          mrs_all[:].unsqueeze(2).broadcast_to([128, NCH, E]),
                          s_sb[:].unsqueeze(1).broadcast_to([128, NCH, E]),
                          ALU.mult).then_inc(s_ve, 1)
                      vector.tensor_sub(lfin[:],
                                        lfin[:],
                                        v1t[:, 0, 0:128].rearrange("p (j e) -> p j e", e=E)
                                        ).then_inc(s_ve, 1)
                      vector.tensor_copy(lfin_dbg[:], lfin[:]).then_inc(s_ve, 1)
                      vector.tensor_copy(rs_dbg[:], rs_all[:]).then_inc(s_ve, 1)
                      vector.tensor_copy(v1_dbg[:], v1t[:, (NCH - 1) % 2, :]).then_inc(s_ve, 1)
                  n["ve"] += NCH * VE_C + 6
                  P3_VE = n["ve"]

                  @block.scalar
                  def _(scalar):
                      for j in range(NCH):
                          scalar.wait_ge(s_ve, ve0 + j * VE_C + 11)
                          scalar.activation(ln_scr[:], v1t[:, j % 2, :],
                                            ACT.Square, bias=0.0,
                                            scale=1.0 / 32.0, accum_out=t_a[:]
                                            ).then_inc(s_se, 1)
                          scalar.wait_ge(s_ve, ve0 + j * VE_C + 12)
                          scalar.mul(t_mu[:], s1m[:], 1.0 / 1024.0
                                     ).then_inc(s_se, 1)
                          scalar.wait_ge(s_ve, ve0 + j * VE_C + 13)
                          scalar.activation(sdt[:], v_b[:], ACT.Sqrt,
                                            bias=epsA[:], scale=-1.0
                                            ).then_inc(s_se, 1)
                          scalar.wait_ge(s_ve, ve0 + j * VE_C + 14)
                          scalar.copy(rsig2[:], rsig[:]).then_inc(s_se, 1)
                          scalar.wait_ge(s_ve, ve0 + j * VE_C + 16)
                          scalar.dma_start(out=z1hbm[128 * j:128 * j + 128, :],
                                           in_=z1t[:, j % 2, :]).then_inc(s_st, 16)
                          scalar.dma_start(out=x_src[128 * j:128 * j + 128, :],
                                           in_=z1bt[:, j % 2, :]).then_inc(s_st, 16)
                      scalar.wait_ge(s_ve, P3_VE)
                      scalar.dma_start(out=l_src.rearrange("(j p) e -> p j e", p=128),
                                       in_=lfin[:]).then_inc(s_st, 16)
                  n["se"] += NCH * SE_C
                  n["st"] += NCH * 2 * 16 + 16
                  n["te"] += NCH * (KD + 1)
                  P3_ST = n["st"]
                  P3_TE = n["te"]

            @block.gpsimd
            def _(gpsimd):
                gpsimd.wait_ge(s_st, P3_ST - 16 - 8 * 32)
                gpsimd.collective_compute("AllGather", ALU.bypass,
                                          replica_groups=RG,
                                          ins=[x_src[0:1024, :]],
                                          outs=[x_ag[0:8192, :]]).then_inc(s_cc, 1)
                gpsimd.wait_ge(s_st, P3_ST)
                gpsimd.collective_compute("AllGather", ALU.bypass,
                                          replica_groups=RG, ins=[l_src],
                                          outs=[l_ag]).then_inc(s_cc, 1)
                gpsimd.collective_compute("AllGather", ALU.bypass,
                                          replica_groups=RG,
                                          ins=[x_src[1024:2048, :]],
                                          outs=[x_ag[8192:16384, :]]).then_inc(s_cc, 1)
            n["cc"] += 3

            if upto <= 3:
                _nve = n["ve"]; _nldg = n["ldg"]; _ndg = n["dg"]; _nte = n["te"]
                @block.scalar
                def _(scalar):
                    scalar.wait_ge(s_ve, _nve)
                    scalar.wait_ge(s_te, _nte)
                    scalar.dma_start(out=out_h[0:128, :], in_=bb[:]).then_inc(s_out, 16)
                    scalar.dma_start(out=out_h[128:256, :], in_=v1_dbg[:]).then_inc(s_out, 16)
                    scalar.dma_start(out=out_h[256:384, :], in_=z1hbm[1920:2048, :]).then_inc(s_out, 16)
                    scalar.dma_start(out=out_h[384:512, 0:128],
                                     in_=lfin_dbg.rearrange("p a b -> p (a b)")).then_inc(s_out, 16)
                    scalar.dma_start(out=out_h[512:640, 0:16], in_=rs_dbg[:]).then_inc(s_out, 16)
                    scalar.wait_ge(s_out, 80)
                @block.gpsimd
                def _(gpsimd):
                    gpsimd.wait_ge(s_ldg, _nldg)
                    gpsimd.wait_ge(s_dg, _ndg)
                    gpsimd.wait_ge(s_ve, _nve)
                    gpsimd.dma_start(out=out_h[640:768, :], in_=x_src[0:128, :]).then_inc(s_ldg, 16)
                    gpsimd.wait_ge(s_ldg, _nldg + 16)
                mybir.codegen_inst_isa_subclasses(nc)
                return nc

            with (
                nc.sbuf_tensor("Lg", [128, 128, E], F32) as Lg,
                nc.sbuf_tensor("r_a", [128, 128], F32) as r_a,
                nc.sbuf_tensor("r_b", [128, 128], F32) as r_b,
                nc.sbuf_tensor("r_3a", [128, E, 128], F32) as r_3a,
                nc.sbuf_tensor("r_3b", [128, E, 128], F32) as r_3b,
                nc.sbuf_tensor("eidx", [128, 128], F32) as eidx,
                nc.sbuf_tensor("ptok", [128, 128], F32) as ptok,
                nc.sbuf_tensor("keepf", [128, 128], F32) as keepf,
                nc.sbuf_tensor("gidxf", [128, 128], F32) as gidxf,
                nc.sbuf_tensor("scaleg", [128, 128], F32) as scaleg,
                nc.sbuf_tensor("scidx", [128, 128], I16) as scidx,
                nc.sbuf_tensor("val_t", [128, 16, 64], F32) as val_t,
                nc.sbuf_tensor("ztile", [128, 1056], F32) as ztile,
                nc.sbuf_tensor("xg", [128, 16, 64], F32) as xg,
                nc.sbuf_tensor("xcol", [128, 16], F32) as xcol,
                nc.psum_tensor("p4_a", [128, 2, 512], F32) as p4_a,
                nc.psum_tensor("p4_b", [128, E], F32) as p4_b,
            ):
                @block.sync
                def _(sync):
                    sync.wait_ge(s_cc, n["cc"] - 1)
                    sync.dma_start(out=Lg[:],
                                   in_=l_ag.rearrange("(p j) e -> p j e", p=128)
                                   ).then_inc(s_ldw, 16)
                n["ldw"] += 16
                P4_LG = n["ldw"]

                ve0, se0, te0 = n["ve"], n["se"], n["te"]

                @block.vector
                def _(vector):
                    vector.memset(xidx[:], 0)
                    vector.memset(cidx[:], 0)
                    vector.memset(scidx[:], 0)
                    vector.memset(val_t[:], 0.0)
                    vector.memset(ztile[:], 0.0)
                    vector.wait_ge(s_ldw, P4_LG)
                    vector.reduce_max(r_a[:].unsqueeze(2), Lg[:], axis=AX.X
                                      ).then_inc(s_ve, 1)
                    vector.tensor_tensor(
                        r_3a[:], Lg.transpose([0, 2, 1]),
                        r_a[:].unsqueeze(1).broadcast_to([128, E, 128]),
                        ALU.subtract).then_inc(s_ve, 1)
                    vector.wait_ge(s_se, se0 + 1)
                    vector.tensor_reduce(r_b[:].unsqueeze(2),
                                         r_3b.transpose([0, 2, 1]),
                                         axis=AX.X, op=ALU.add).then_inc(s_ve, 1)
                    vector.reciprocal(ptok[:], r_b[:]).then_inc(s_ve, 1)
                    vector.tensor_tensor(
                        r_3a[:], Lg.transpose([0, 2, 1]),
                        r_a[:].unsqueeze(1).broadcast_to([128, E, 128]),
                        ALU.is_equal).then_inc(s_ve, 1)
                    vector.scalar_tensor_tensor(
                        r_3a[:], r_3a[:], -999.0,
                        iota8[:].unsqueeze(2).broadcast_to([128, E, 128]),
                        op0=ALU.mult, op1=ALU.add).then_inc(s_ve, 1)
                    vector.tensor_reduce(eidx[:].unsqueeze(2),
                                         r_3a.transpose([0, 2, 1]),
                                         axis=AX.X, op=ALU.min).then_inc(s_ve, 1)
                    vector.tensor_scalar_add(eidx[:], eidx[:], 999.0
                                             ).then_inc(s_ve, 1)
                    vector.tensor_tensor(
                        r_3a[:],
                        iota8[:].unsqueeze(2).broadcast_to([128, E, 128]),
                        eidx[:].unsqueeze(1).broadcast_to([128, E, 128]),
                        ALU.is_equal).then_inc(s_ve, 1)
                    vector.engine_nop().then_inc(s_ve, 1)
                    for e_ in range(E):
                        vector.tensor_tensor_scan(r_3b[:, e_, :], r_3a[:, e_, :],
                                                  ztile[:, 0:128], 0.0, op0=ALU.add,
                                                  op1=ALU.add).then_inc(s_ve, 1)
                    with nc.allow_low_precision(reason="counts <= 128 exact in bf16"):
                        vector.tensor_reduce(t8b[:].unsqueeze(2), r_3a[:], axis=AX.X,
                                             op=ALU.add).then_inc(s_ve, 1)
                    if upto < 3.2:
                        return
                    vector.wait_ge(s_se, se0 + 2)
                    vector.engine_nop().then_inc(s_ve, 1)
                    if upto < 3.3:
                        return
                    vector.tensor_tensor(r_3b[:], r_3b[:],
                                         b_sb[:].unsqueeze(2).broadcast_to(
                                             [128, E, 128]),
                                         ALU.add).then_inc(s_ve, 1)
                    vector.tensor_mul(r_3b[:], r_3b[:], r_3a[:]).then_inc(s_ve, 1)
                    vector.tensor_add(r_3b[:, 0:4, :], r_3b[:, 0:4, :],
                                      r_3b[:, 4:8, :]).then_inc(s_ve, 1)
                    vector.tensor_add(r_3b[:, 0:2, :], r_3b[:, 0:2, :],
                                      r_3b[:, 2:4, :]).then_inc(s_ve, 1)
                    vector.tensor_add(r_a[:], r_3b[:, 0, :], r_3b[:, 1, :]
                                      ).then_inc(s_ve, 1)
                    vector.tensor_single_scalar(keepf[:], r_a[:], 2048.0, ALU.is_le
                                                ).then_inc(s_ve, 1)
                    vector.tensor_mul(scaleg[:], ptok[:], keepf[:]).then_inc(s_ve, 1)
                    vector.tensor_scalar(r_b[:], r_a[:], -1.0, 2047.0,
                                         op0=ALU.add, op1=ALU.min).then_inc(s_ve, 1)
                    vector.tensor_single_scalar(r_3a[:, 0, :], r_b[:], 512.0,
                                                ALU.is_ge)
                    vector.tensor_single_scalar(r_3a[:, 1, :], r_b[:], 1024.0,
                                                ALU.is_ge)
                    vector.tensor_single_scalar(r_3a[:, 2, :], r_b[:], 1536.0,
                                                ALU.is_ge)
                    vector.tensor_add(r_3a[:, 0, :], r_3a[:, 0, :], r_3a[:, 1, :])
                    vector.tensor_add(r_3a[:, 0, :], r_3a[:, 0, :], r_3a[:, 2, :])
                    vector.scalar_tensor_tensor(gidxf[:], eidx[:], 512.0, r_b[:],
                                                op0=ALU.mult, op1=ALU.add)
                    vector.scalar_tensor_tensor(gidxf[:], r_3a[:, 0, :], 3584.0,
                                                gidxf[:], op0=ALU.mult,
                                                op1=ALU.add).then_inc(s_ve, 1)
                    vector.tensor_scalar(r_b[:], r_a[:], -1.0, 2048.0,
                                         op0=ALU.add, op1=ALU.min).then_inc(s_ve, 1)
                    vector.scalar_tensor_tensor(keepf[:], eidx[:], 2056.0, r_b[:],
                                                op0=ALU.mult, op1=ALU.add
                                                ).then_inc(s_ve, 1)
                    vector.scalar_tensor_tensor(
                        val_t[:, :, 0].unsqueeze(2),
                        myexp[:, 0:1].unsqueeze(2).broadcast_to([128, 16, 1]),
                        1024.0, giotat[:].unsqueeze(2),
                        op0=ALU.mult, op1=ALU.add).then_inc(s_ve, 1)
                    if upto < 3.4:
                        return

                @block.scalar
                def _(scalar):
                    scalar.wait_ge(s_ve, ve0 + 2)
                    scalar.activation(r_3b[:], r_3a[:], ACT.Exp).then_inc(s_se, 1)
                    if upto >= 3.2:
                        scalar.wait_ge(s_te, te0 + 1)
                        scalar.copy(b_sb[:], p4_b[:]).then_inc(s_se, 1)
                n["se"] += 1 + (1 if upto >= 3.2 else 0)

                @block.tensor
                def _(tensor):
                    if upto < 3.2:
                        return
                    tensor.wait_ge(s_ve, ve0 + 19)
                    tensor.matmul(p4_b[:], sltb[:], t8b[:], start=True, stop=True
                                  ).then_inc(s_te, 1)
                    if upto < 3.4:
                        return
                    tensor.wait_ge(s_ve, ve0 + 32)
                    for q in range(8):
                        if q >= 2:
                            tensor.wait_ge(s_ve, ve0 + 32 + q - 1)
                        tensor.transpose(p4_a[0:16, q % 2, 0:128],
                                         keepf[:, 16 * q:16 * q + 16], ident[:]
                                         ).then_inc(s_te, 1)
                    for q in range(8):
                        tensor.wait_ge(s_ve, ve0 + 32 + 6 + q + 1)
                        tensor.transpose(p4_a[0:16, q % 2, 0:128],
                                         gidxf[:, 16 * q:16 * q + 16], ident[:]
                                         ).then_inc(s_te, 1)
                    tensor.wait_ge(s_ve, ve0 + 32 + 14 + 2)
                    tensor.transpose(p4_a[:, 0, 0:128], scaleg[:], ident[:]
                                     ).then_inc(s_te, 1)

                @block.vector
                def _(vector):
                    if upto < 3.4:
                        return
                    for q in range(8):
                        vector.wait_ge(s_te, te0 + 1 + q + 1)
                        vector.tensor_copy(
                            m16[0:16, :].rearrange("p (c e) -> p c e", e=8)[:, :, q],
                            p4_a[0:16, q % 2, 0:128]).then_inc(s_ve, 1)
                    for q in range(8):
                        vector.wait_ge(s_te, te0 + 9 + q + 1)
                        vector.tensor_copy(
                            g16[0:16, :].rearrange("p (c e) -> p c e", e=8)[:, :, q],
                            p4_a[0:16, q % 2, 0:128]).then_inc(s_ve, 1)
                    vector.wait_ge(s_te, te0 + 18)
                    if upto < 3.5:
                        return
                    creg = vector.value_load(coreid[:], min_val=0, max_val=7)
                    vector.tensor_copy(r_b[:], p4_a[:, 0, 0:128])
                    vector.tensor_copy(scalec[:],
                                       r_b[:][:, bass.ts(creg, 16)]
                                       ).then_inc(s_ve, 1)
                    vector.tensor_copy(cidx[0:16, :],
                                       g16[0:16, :][:, bass.ts(creg, 128)]
                                       ).then_inc(s_ve, 1)
                    vector.tensor_copy(scidx[0:16, :],
                                       m16[0:16, :][:, bass.ts(creg, 128)]
                                       ).then_inc(s_ve, 1)
                n["ve"] += (19 + (1 if upto >= 3.2 else 0)
                            + (12 if upto >= 3.3 else 0)
                            + (16 if upto >= 3.4 else 0)
                            + (3 if upto >= 3.5 else 0))
                P4_VE = n["ve"]
                n["te"] += ((1 if upto >= 3.2 else 0)
                            + (17 if upto >= 3.4 else 0))
                P4_TE = n["te"]

                if upto < 3.6:
                    _nve = n["ve"]; _nldg = n["ldg"]; _ndg = n["dg"]; _nte = n["te"]
                    @block.scalar
                    def _(scalar):
                        scalar.wait_ge(s_ve, _nve)
                        scalar.wait_ge(s_te, _nte)
                        scalar.dma_start(out=out_h[0:128, :], in_=bb[:]).then_inc(s_out, 16)
                        scalar.wait_ge(s_out, 16)
                    @block.gpsimd
                    def _(gpsimd):
                        gpsimd.wait_ge(s_ldg, _nldg)
                        gpsimd.wait_ge(s_dg, _ndg)
                    mybir.codegen_inst_isa_subclasses(nc)
                    return nc

                GELEMS = GRID_ROWS * 64
                CH = 128 * 1056

                @block.gpsimd
                def _(gpsimd):
                    gpsimd.load_library(_LIB_MLP)
                    gpsimd.wait_ge(s_ve, ve0 + 1)
                    off = 0
                    nz = 0
                    while off < GELEMS:
                        this = min(CH, GELEMS - off)
                        gpsimd.dma_start(
                            out=grid_fl[off:off + this].rearrange(
                                "(p f) -> p f", p=128),
                            in_=ztile[:, 0:this // 128]).then_inc(s_ldg, 16)
                        off += this
                        nz += 1
                    gpsimd.wait_ge(s_ve, P4_VE)
                    gpsimd.dma_start(out=scidx[16:32, :], in_=scidx[0:16, :]
                                     ).then_inc(s_ldg, 16)
                    gpsimd.dma_start(out=cidx[16:32, :], in_=cidx[0:16, :]
                                     ).then_inc(s_ldg, 16)
                    gpsimd.wait_ge(s_ldg, n["ldg"] + (nz + 2) * 16)
                    gpsimd.dma_scatter_add(
                        grid_fl.rearrange("(r c) -> r c", c=64),
                        val_t[:], scidx[:], num_idxs=T, num_idxs_reg=T,
                        elem_size=64).then_inc(s_dg, 16)
                    gpsimd.wait_ge(s_dg, n["dg"] + 16)
                    gpsimd.collective_compute(
                        "ReduceScatter", ALU.add, replica_groups=RG,
                        ins=[grid_fl.rearrange("(r c) -> r c", c=64)],
                        outs=[rs_out]).then_inc(s_cc, 1)
                    gpsimd.wait_ge(s_cc, n["cc"] + 1)
                    gpsimd.dma_start(
                        out=xg[:],
                        in_=rs_out[0:CAP, :].rearrange("(p r) c -> p r c", p=128)
                        ).then_inc(s_ldg, 16)
                n["ldg"] += (8 + 2 + 1) * 16
                n["dg"] += 16
                n["cc"] += 1
                P4_RB = n["ldg"]

                @block.vector
                def _(vector):
                    vector.wait_ge(s_ldg, P4_RB)
                    vector.tensor_copy(xcol[:], xg[:, :, 0]).then_inc(s_ve, 1)
                n["ve"] += 1

                @block.tensor
                def _(tensor):
                    tensor.wait_ge(s_ve, n["ve"])
                    tensor.transpose(p4_a[0:16, 0, 0:128], xcol[:], ident[:]
                                     ).then_inc(s_te, 1)
                n["te"] += 1

                @block.vector
                def _(vector):
                    vector.wait_ge(s_te, n["te"])
                    vector.tensor_copy(xidx[0:16, :], p4_a[0:16, 0, 0:128]
                                       ).then_inc(s_ve, 1)
                n["ve"] += 1
                P4_VE = n["ve"]

                @block.gpsimd
                def _(gpsimd):
                    gpsimd.wait_ge(s_ve, P4_VE)
                    gpsimd.dma_start(out=xidx[16:32, :], in_=xidx[0:16, :]
                                     ).then_inc(s_ldg, 16)
                n["ldg"] += 16
                P4_LDG = n["ldg"]
                P4_CC = n["cc"]

                if upto <= 4:
                    _nve = n["ve"]; _nldg = n["ldg"]; _ndg = n["dg"]; _nte = n["te"]
                    @block.scalar
                    def _(scalar):
                        scalar.wait_ge(s_ve, _nve)
                        scalar.wait_ge(s_te, _nte)
                        scalar.dma_start(out=out_h[0:128, :], in_=bb[:]).then_inc(s_out, 16)
                        scalar.dma_start(out=out_h[256:384, :], in_=xg.rearrange("p a b -> p (a b)")).then_inc(s_out, 16)
                        scalar.dma_start(out=out_h[384:512, 0:16], in_=xcol[:]).then_inc(s_out, 16)
                        scalar.dma_start(out=out_h[512:640, 0:16], in_=scalec[:]).then_inc(s_out, 16)
                        scalar.wait_ge(s_out, 64)
                    @block.gpsimd
                    def _(gpsimd):
                        gpsimd.wait_ge(s_ldg, _nldg)
                        gpsimd.wait_ge(s_dg, _ndg)
                        gpsimd.wait_ge(s_ve, _nve)
                        gpsimd.dma_start(out=out_h[128:256, 0:128], in_=xidx[:]).then_inc(s_ldg, 16)
                        gpsimd.dma_start(out=out_h[640:768, 0:128], in_=cidx[:]).then_inc(s_ldg, 16)
                        gpsimd.dma_start(out=out_h[768:896, 0:128], in_=gidxf[:]).then_inc(s_ldg, 16)
                        gpsimd.dma_start(out=out_h[896:1024, 0:128], in_=keepf[:]).then_inc(s_ldg, 16)
                        gpsimd.dma_start(out=out_h[1024:1152, :],
                                         in_=Lg.rearrange("p a b -> p (a b)")).then_inc(s_ldg, 16)
                        gpsimd.dma_start(out=out_h[1152:1280, 0:128],
                                         in_=lfin_dbg.rearrange("p a b -> p (a b)")).then_inc(s_ldg, 16)
                        gpsimd.dma_start(out=out_h[1280:1408, 0:16], in_=rs_dbg[:]).then_inc(s_ldg, 16)
                        gpsimd.dma_start(out=out_h[1408:1536, 0:128].rearrange("p (j e) -> p j e", e=E),
                                         in_=l_ag[0:2048, :].rearrange("(j p) e -> p j e", p=128)
                                         ).then_inc(s_ldg, 16)
                        gpsimd.wait_ge(s_ldg, _nldg + 128)
                    mybir.codegen_inst_isa_subclasses(nc)
                    return nc

            with (
                nc.sbuf_tensor("xeT", [128, KD, 512], BF16) as xeT,
                nc.sbuf_tensor("hTt", [128, 16, 512], BF16) as hTt,
                nc.sbuf_tensor("yeb", [128, 4, D], F32) as yeb,
                nc.psum_tensor("h_ps", [128, 2, 512], F32) as h_ps,
                nc.psum_tensor("ye_ps", [128, 2, 512], F32) as ye_ps,
            ):
                W_RDY = [P3_W12]
                ST_H0 = [0]
                LDGB = []
                if upto < 4.8:
                    _blocks = [(0, 0)]
                elif upto < 5:
                    _blocks = [(0, b) for b in range(4)]
                else:
                    _blocks = None
                for half in range(2):
                    if _blocks is not None and all(h != half for h, _ in _blocks):
                        continue
                    if half == 1:
                        te_snap = n["te"]
                        ST_H0[0] = n["st"]

                        @block.gpsimd
                        def _(gpsimd):
                            gpsimd.wait_ge(s_te, te_snap)
                            for k in range(KD):
                                gpsimd.dma_start(
                                    out=w1h[:, k, :],
                                    in_=w1_in[128 * k:128 * k + 128, FF // 2:FF]
                                ).then_inc(s_ldg, 16)
                            for m in range(16):
                                gpsimd.dma_start(
                                    out=w2h[:, m, :],
                                    in_=w2_in[128 * (16 + m):128 * (16 + m) + 128, :]
                                ).then_inc(s_ldg, 16)
                        n["ldg"] += (KD + 16) * 16
                        W_RDY[0] = n["ldg"]

                    for b_ in range(4):
                        if _blocks is not None and (half, b_) not in _blocks:
                            continue
                        te0, ve0, se0, dg0 = n["te"], n["ve"], n["se"], n["dg"]
                        st0, ldg0 = n["st"], n["ldg"]
                        wrdy = W_RDY[0]

                        @block.gpsimd
                        def _(gpsimd):
                            if half + b_ > 0:
                                gpsimd.wait_ge(s_te, te0)
                            else:
                                gpsimd.wait_ge(s_ldg, P4_LDG)
                                gpsimd.wait_ge(s_ve, P4_VE)
                                gpsimd.wait_ge(s_te, P4_TE)
                            gpsimd.dma_gather(xeT[:], x_ag,
                                              xidx[:, 32 * b_:32 * b_ + 32],
                                              num_idxs=512, num_idxs_reg=512,
                                              elem_size=D, transpose=True
                                              ).then_inc(s_dg, 16)
                        n["dg"] += 16

                        @block.tensor
                        def _(tensor):
                            if upto < 4.4:
                                return
                            if half + b_ == 0:
                                tensor.wait_ge(s_ve, ve0)
                            tensor.wait_ge(s_dg, dg0 + 16)
                            tensor.wait_ge(s_ldg, wrdy)
                            for m in range(16):
                                if m >= 2:
                                    tensor.wait_ge(s_se, se0 + m - 1)
                                mm = None
                                for k in range(KD):
                                    mm = tensor.matmul(
                                        h_ps[:, m % 2, :],
                                        w1h[:, k, 128 * m:128 * m + 128],
                                        xeT[:, k, :], start=(k == 0),
                                        stop=(k == KD - 1))
                                mm.then_inc(s_te, 1)
                            if upto < 4.6:
                                return
                            tensor.wait_ge(s_se, se0 + 16)
                            i = 0
                            for ss in range(4):
                                for dh in range(2):
                                    if i >= 2:
                                        tensor.wait_ge(s_ve, ve0 + i - 1)
                                    mm = None
                                    for m in range(16):
                                        mm = tensor.matmul(
                                            ye_ps[:, i % 2, :],
                                            hTt[:, m, 128 * ss:128 * ss + 128],
                                            w2h[:, m, 512 * dh:512 * dh + 512],
                                            start=(m == 0), stop=(m == 15))
                                    mm.then_inc(s_te, 1)
                                    i += 1
                        n["te"] += (16 if upto >= 4.4 else 0) + (8 if upto >= 4.6 else 0)

                        @block.scalar
                        def _(scalar):
                            if upto < 4.4:
                                return
                            for m in range(16):
                                scalar.wait_ge(s_te, te0 + m + 1)
                                scalar.activation(hTt[:, m, :], h_ps[:, m % 2, :],
                                                  ACT.Relu).then_inc(s_se, 1)
                            if upto < 4.6:
                                return
                            if half == 0:
                                for ss in range(4):
                                    scalar.wait_ge(s_ve, ve0 + 2 * ss + 2)
                                    scalar.dma_start(
                                        out=ye_src[512 * b_ + 128 * ss:
                                                   512 * b_ + 128 * ss + 128, :],
                                        in_=yeb[:, ss, :]).then_inc(s_st, 16)
                        n["se"] += 16 if upto >= 4.4 else 0
                        if half == 0 and upto >= 4.6:
                            n["st"] += 4 * 16

                        @block.vector
                        def _(vector):
                            if upto < 4.6:
                                return
                            i = 0
                            for ss in range(4):
                                for dh in range(2):
                                    vector.wait_ge(s_te, te0 + 16 + i + 1)
                                    if half == 0 and b_ > 0:
                                        vector.wait_ge(s_st, st0 - 64 + 16 * (ss + 1))
                                    elif half == 1 and b_ == 0:
                                        vector.wait_ge(s_st, ST_H0[0])
                                    elif half == 1:
                                        vector.wait_ge(s_ldg, ldg0 - 64 + 16 * (ss + 1))
                                    vector.tensor_copy(yeb[:, ss, 512 * dh:512 * dh + 512],
                                                       ye_ps[:, i % 2, :]
                                                       ).then_inc(s_ve, 1)
                                    i += 1
                        n["ve"] += 8 if upto >= 4.6 else 0

                        if half == 1:
                            @block.gpsimd
                            def _(gpsimd):
                                if b_ == 0:
                                    gpsimd.wait_ge(s_st, ST_H0[0])
                                for ss in range(4):
                                    gpsimd.wait_ge(s_ve, ve0 + 2 * ss + 2)
                                    gpsimd.dma_start(
                                        out=ye_src[512 * b_ + 128 * ss:
                                                   512 * b_ + 128 * ss + 128, :],
                                        in_=yeb[:, ss, :], accum_op=ALU.add
                                    ).then_inc(s_ldg, 16)
                            n["ldg"] += 4 * 16
                            LDGB.append(n["ldg"])
                            if upto >= 5:
                                _ldgb = n["ldg"]

                                @block.gpsimd
                                def _(gpsimd):
                                    gpsimd.wait_ge(s_ldg, _ldgb)
                                    gpsimd.collective_compute(
                                        "AllGather", ALU.bypass,
                                        replica_groups=RG,
                                        ins=[ye_src[512 * b_:512 * b_ + 512, :]],
                                        outs=[ye_ag[4096 * b_:4096 * b_ + 4096, :]]
                                    ).then_inc(s_cc, 1)
                                n["cc"] += 1
                P5_ST = n["st"]
                P5_LDG = n["ldg"]

                if upto < 5:
                    _nve = n["ve"]; _nldg = n["ldg"]; _ndg = n["dg"]
                    _nte = n["te"]; _nse = n["se"]; _nst = n["st"]
                    @block.scalar
                    def _(scalar):
                        scalar.wait_ge(s_ve, _nve)
                        scalar.wait_ge(s_te, _nte)
                        scalar.wait_ge(s_se, _nse)
                        scalar.wait_ge(s_st, _nst)
                        scalar.dma_start(out=out_h[0:128, :], in_=bb[:]).then_inc(s_out, 16)
                        scalar.wait_ge(s_out, 16)
                    @block.gpsimd
                    def _(gpsimd):
                        gpsimd.wait_ge(s_ldg, _nldg)
                        gpsimd.wait_ge(s_dg, _ndg)
                    mybir.codegen_inst_isa_subclasses(nc)
                    return nc

            P6_CC = n["cc"]

            if upto <= 5:
                _nve = n["ve"]; _nldg = n["ldg"]; _ndg = n["dg"]; _nte = n["te"]
                @block.scalar
                def _(scalar):
                    scalar.wait_ge(s_ve, _nve)
                    scalar.wait_ge(s_te, _nte)
                    scalar.dma_start(out=out_h[0:128, :], in_=bb[:]).then_inc(s_out, 16)
                    scalar.wait_ge(s_out, 16)
                @block.gpsimd
                def _(gpsimd):
                    gpsimd.wait_ge(s_ldg, _nldg)
                    gpsimd.wait_ge(s_dg, _ndg)
                mybir.codegen_inst_isa_subclasses(nc)
                return nc

            with (
                nc.sbuf_tensor("ycomb", [128, 2, 4, D], F32) as ycomb,
                nc.sbuf_tensor("zb1", [128, 2, 4, D], F32) as zb1,
                nc.sbuf_tensor("o_t", [128, 2, D], F32) as o_t,
                nc.sbuf_tensor("s1b", [128, 1], F32) as s1b,
                nc.sbuf_tensor("tmub", [128, 1], F32) as tmub,
                nc.sbuf_tensor("tab", [128, 1], F32) as tab,
                nc.sbuf_tensor("vbb", [128, 1], F32) as vbb,
                nc.sbuf_tensor("sdb", [128, 1], F32) as sdb,
                nc.sbuf_tensor("rsb", [128, 1], F32) as rsb,
                nc.sbuf_tensor("rsb2", [128, 1], F32) as rsb2,
                nc.sbuf_tensor("scr6", [128, D], F32) as scr6,
            ):
                ldw_z0 = n["ldw"]

                ve0_p6 = n["ve"]

                @block.sync
                def _(sync):
                    sync.wait_ge(s_cc, P6_CC)
                    for cb in range(4):
                        if cb >= 2:
                            sync.wait_ge(s_ve, ve0_p6 + (cb - 2) * 20 + 16)
                        sync.dma_start(
                            out=zb1[:, cb % 2, :, :],
                            in_=z1hbm[512 * cb:512 * cb + 512, :].rearrange(
                                "(u p) d -> p u d", p=128)).then_inc(s_ldw, 16)
                n["ldw"] += 4 * 16

                dg_c0 = n["dg"]

                @block.gpsimd
                def _(gpsimd):
                    gpsimd.wait_ge(s_ldg, P4_LDG)
                    gpsimd.wait_ge(s_cc, P6_CC)
                    for cb in range(4):
                        if cb >= 2:
                            gpsimd.wait_ge(s_ve, ve0_p6 + (cb - 2) * 20 + 20)
                        gpsimd.dma_gather(ycomb[:, cb % 2, :, :], ye_ag,
                                          cidx[:, 32 * cb:32 * cb + 32],
                                          num_idxs=512, num_idxs_reg=512,
                                          elem_size=D).then_inc(s_dg, 16)
                n["dg"] += 4 * 16

                ve0, se0 = n["ve"], n["se"]
                VE_U = 5

                @block.vector
                def _(vector):
                    for cb in range(4):
                        vector.wait_ge(s_dg, dg_c0 + (cb + 1) * 16)
                        vector.wait_ge(s_ldw, ldw_z0 + (cb + 1) * 16)
                        for u in range(4):
                            uu = 4 * cb + u
                            vector.scalar_tensor_tensor(
                                ycomb[:, cb % 2, u, :], ycomb[:, cb % 2, u, :],
                                scalec[:, uu:uu + 1], zb1[:, cb % 2, u, :],
                                op0=ALU.mult, op1=ALU.add).then_inc(s_ve, 1)
                            vector.tensor_reduce(
                                s1b[:].unsqueeze(2),
                                ycomb[:, cb % 2, u, :].unsqueeze(1),
                                axis=AX.X, op=ALU.add).then_inc(s_ve, 1)
                            vector.wait_ge(s_se, se0 + uu * 4 + 2)
                            vector.scalar_tensor_tensor(
                                vbb[:], tmub[:], tmub[:], tab[:],
                                op0=ALU.mult, op1=ALU.subtract
                                ).then_inc(s_ve, 1)
                            vector.wait_ge(s_se, se0 + uu * 4 + 3)
                            vector.reciprocal(rsb[:], sdb[:]).then_inc(s_ve, 1)
                            vector.wait_ge(s_se, se0 + uu * 4 + 4)
                            if uu >= 2:
                                vector.wait_ge(s_out, (uu - 1) * 16)
                            vector.tensor_scalar(o_t[:, uu % 2, :],
                                                 ycomb[:, cb % 2, u, :],
                                                 tmub[:], rsb2[:],
                                                 op0=ALU.subtract, op1=ALU.mult
                                                 ).then_inc(s_ve, 1)
                n["ve"] += 4 * 4 * VE_U

                @block.scalar
                def _(scalar):
                    for cb in range(4):
                        for u in range(4):
                            uu = 4 * cb + u
                            scalar.wait_ge(s_ve, ve0 + uu * VE_U + 1)
                            scalar.activation(scr6[:], ycomb[:, cb % 2, u, :],
                                              ACT.Square, bias=0.0,
                                              scale=1.0 / 32.0,
                                              accum_out=tab[:]).then_inc(s_se, 1)
                            scalar.wait_ge(s_ve, ve0 + uu * VE_U + 2)
                            scalar.mul(tmub[:], s1b[:], 1.0 / 1024.0
                                       ).then_inc(s_se, 1)
                            scalar.wait_ge(s_ve, ve0 + uu * VE_U + 3)
                            scalar.activation(sdb[:], vbb[:], ACT.Sqrt,
                                              bias=epsA[:], scale=-1.0
                                              ).then_inc(s_se, 1)
                            scalar.wait_ge(s_ve, ve0 + uu * VE_U + 4)
                            scalar.copy(rsb2[:], rsb[:]).then_inc(s_se, 1)
                            scalar.wait_ge(s_ve, ve0 + uu * VE_U + 5)
                            scalar.dma_start(
                                out=out_h[512 * cb + 128 * u:
                                          512 * cb + 128 * u + 128, :],
                                in_=o_t[:, uu % 2, :]).then_inc(s_out, 16)
                    scalar.wait_ge(s_out, 16 * 16)
                n["se"] += 16 * 4
                n["out"] += 16 * 16

                @block.gpsimd
                def _(gpsimd):
                    gpsimd.wait_ge(s_dg, n["dg"])

    mybir.codegen_inst_isa_subclasses(nc)
    return nc


def _consts():
    ident = np.eye(128, dtype=np.float32)
    slt = np.triu(np.ones((128, 128), np.float32), 1)
    iota8 = np.tile(np.arange(E, dtype=np.float32), (128, 1))
    giota1 = (np.arange(128, dtype=np.float32)[:, None] * 128
              + np.arange(128, dtype=np.float32)[None, :] + 1.0)
    giotat = (np.arange(128, dtype=np.float32)[:, None]
              + 128.0 * np.arange(16, dtype=np.float32)[None, :]
              + 7168.0 * (np.arange(16, dtype=np.float32)[None, :] >= 8))
    ones_r = np.ones((1, 128), np.float32)
    ones_c = np.ones((128, 1), np.float32)
    return ident, slt, iota8, giota1, giotat, ones_r, ones_c


def kernel(L0_attn_x, L0_attn_W_qkv, L0_attn_proj, L0_attn_W_out, L0_attn_b_out,
           L0_ffn_Wr, L0_ffn_W1, L0_ffn_W2):
    x = np.ascontiguousarray(np.asarray(L0_attn_x, dtype=np.float32))
    wqkv = np.asarray(L0_attn_W_qkv, dtype=np.float32)
    proj = np.ascontiguousarray(np.asarray(L0_attn_proj, dtype=np.float32))
    wout = np.ascontiguousarray(np.asarray(L0_attn_W_out, dtype=np.float32))
    bout = np.ascontiguousarray(np.asarray(L0_attn_b_out, dtype=np.float32)
                                .reshape(1, D))
    wr = np.ascontiguousarray(np.asarray(L0_ffn_Wr, dtype=np.float32))
    w1 = np.asarray(L0_ffn_W1, dtype=np.float32)
    w2 = np.asarray(L0_ffn_W2, dtype=np.float32)

    wqkvT = np.ascontiguousarray(wqkv.T)
    woutT = np.ascontiguousarray(wout.T)
    ident, slt, iota8, giota1, giotat, ones_r, ones_c = _consts()

    if "nc" not in _CACHE:
        _CACHE["nc"] = build()
    nc = _CACHE["nc"]

    in_maps = []
    for c in range(NCORE):
        in_maps.append({
            "x_nat": np.ascontiguousarray(x[c]),
            "xT_in": np.ascontiguousarray(x[c].T),
            "wqkvT": wqkvT,
            "proj_in": proj,
            "woutT_in": woutT,
            "wout_n": wout,
            "bout_in": bout,
            "wr_in": wr,
            "w1_in": np.ascontiguousarray(w1[c]),
            "w2_in": np.ascontiguousarray(w2[c]),
            "ident_in": ident,
            "slt_in": slt,
            "iota8_in": iota8,
            "giota1_in": giota1,
            "giotat_in": giotat,
            "ones_r": ones_r,
            "ones_c": ones_c,
            "myexp_in": np.full((128, 1), c, np.float32),
            "coreid_in": np.array([[c]], np.uint32),
        })
    res = run_bass_kernel_spmd(nc, in_maps, core_ids=list(range(NCORE)))
    return np.stack([res.results[c]["out"] for c in range(NCORE)], axis=0)


# version 12 (final)
# speedup vs baseline: 1.2719x; 1.0206x over previous
import numpy as np
import concourse.bass as bass
import concourse.mybir as mybir
from concourse.bass_utils import run_bass_kernel_spmd
from concourse.library_config import mlp as _LIB_MLP
from concourse.replica_groups import maybe_share_collective_output_space

F32 = mybir.dt.float32
BF16 = mybir.dt.bfloat16
I16 = mybir.dt.int16
U32 = mybir.dt.uint32
AX = mybir.AxisListType
ALU = mybir.AluOpType
ACT = mybir.ActivationFunctionType

NCORE = 8
T = 2048
D = 1024
H = 16
HD = 64
PF = 128
FF = 4096
E = 8
CAP = 2048
TG = NCORE * T
NCH = T // 128
KD = D // 128
RG = [[0, 1, 2, 3, 4, 5, 6, 7]]
SLOTP = 2056
GRID_ROWS = E * SLOTP

_CACHE = {}


def build(upto=6):
    nc = bass.Bass("TRN2", target_bir_lowering=False, detect_race_conditions=False,
                   num_devices=NCORE)

    x_nat = nc.dram_tensor("x_nat", [T, D], F32, kind="ExternalInput").ap()
    xT_in = nc.dram_tensor("xT_in", [D, T], F32, kind="ExternalInput").ap()
    wqkvT = nc.dram_tensor("wqkvT", [D, 3 * D], F32, kind="ExternalInput").ap()
    proj_in = nc.dram_tensor("proj_in", [H, HD, PF], F32, kind="ExternalInput").ap()
    woutT_in = nc.dram_tensor("woutT_in", [D, D], F32, kind="ExternalInput").ap()
    wout_n = nc.dram_tensor("wout_n", [D, D], F32, kind="ExternalInput").ap()
    bout_in = nc.dram_tensor("bout_in", [1, D], F32, kind="ExternalInput").ap()
    wr_in = nc.dram_tensor("wr_in", [D, E], F32, kind="ExternalInput").ap()
    w1_in = nc.dram_tensor("w1_in", [D, FF], F32, kind="ExternalInput").ap()
    w2_in = nc.dram_tensor("w2_in", [FF, D], F32, kind="ExternalInput").ap()
    ident_in = nc.dram_tensor("ident_in", [128, 128], F32, kind="ExternalInput").ap()
    slt_in = nc.dram_tensor("slt_in", [128, 128], F32, kind="ExternalInput").ap()
    iota8_in = nc.dram_tensor("iota8_in", [128, E], F32, kind="ExternalInput").ap()
    giota1_in = nc.dram_tensor("giota1_in", [128, 128], F32, kind="ExternalInput").ap()
    giotat_in = nc.dram_tensor("giotat_in", [128, 16], F32, kind="ExternalInput").ap()
    ones_r = nc.dram_tensor("ones_r", [1, 128], F32, kind="ExternalInput").ap()
    ones_c = nc.dram_tensor("ones_c", [128, 1], F32, kind="ExternalInput").ap()
    myexp_in = nc.dram_tensor("myexp_in", [128, 1], F32, kind="ExternalInput").ap()
    coreid_in = nc.dram_tensor("coreid_in", [1, 1], U32, kind="ExternalInput").ap()

    out_h = nc.dram_tensor("out", [T, D], F32, kind="ExternalOutput").ap()

    z1hbm = nc.dram_tensor("z1hbm", [T, D], F32, kind="Internal").ap()
    l_src = nc.dram_tensor("l_src", [T, E], F32, kind="Internal").ap()
    x_src = nc.dram_tensor("x_src", [T, D], BF16, kind="Internal").ap()
    ye_src = nc.dram_tensor("ye_src", [CAP, D], F32, kind="Internal").ap()
    sp = maybe_share_collective_output_space("AllGather", RG)
    l_ag = nc.dram_tensor("l_ag", [TG, E], F32, kind="Internal", addr_space=sp).ap()
    x_ag = nc.dram_tensor("x_ag", [TG, D], BF16, kind="Internal", addr_space=sp).ap()
    ye_ag = nc.dram_tensor("ye_ag", [E * CAP, D], F32, kind="Internal",
                           addr_space=sp).ap()
    grid_fl = nc.dram_tensor("grid_fl", [GRID_ROWS * 64], F32, kind="Internal").ap()
    rs_out = nc.dram_tensor("rs_out", [GRID_ROWS // NCORE, 64], F32,
                            kind="Internal").ap()

    from contextlib import ExitStack
    with ExitStack() as _stk:
        ident = _stk.enter_context(nc.sbuf_tensor("ident", [128, 128], F32))
        identb = _stk.enter_context(nc.sbuf_tensor("identb", [128, 128], BF16))
        sltb = _stk.enter_context(nc.sbuf_tensor("sltb", [128, 128], BF16))
        iota8 = _stk.enter_context(nc.sbuf_tensor("iota8", [128, E], F32))
        giota1 = _stk.enter_context(nc.sbuf_tensor("giota1", [128, 128], F32))
        giotat = _stk.enter_context(nc.sbuf_tensor("giotat", [128, 16], F32))
        lfin_dbg = _stk.enter_context(nc.sbuf_tensor("lfin_dbg", [128, NCH, E], F32))
        v1_dbg = _stk.enter_context(nc.sbuf_tensor("v1_dbg", [128, D], F32))
        rs_dbg = _stk.enter_context(nc.sbuf_tensor("rs_dbg", [128, 16], F32))
        onesr = _stk.enter_context(nc.sbuf_tensor("onesr", [1, 128], F32))
        onesrb = _stk.enter_context(nc.sbuf_tensor("onesrb", [1, 128], BF16))
        onesc = _stk.enter_context(nc.sbuf_tensor("onesc", [128, 1], F32))
        myexp = _stk.enter_context(nc.sbuf_tensor("myexp", [128, 1], F32))
        coreid = _stk.enter_context(nc.sbuf_tensor("coreid", [1, 1], U32))
        boutr = _stk.enter_context(nc.sbuf_tensor("boutr", [1, D], F32))
        wrf = _stk.enter_context(nc.sbuf_tensor("wrf", [128, KD, E], F32))
        wrb = _stk.enter_context(nc.sbuf_tensor("wrb", [128, KD, E], BF16))
        m_sb = _stk.enter_context(nc.sbuf_tensor("m_sb", [128, KD, E], BF16))
        bwr = _stk.enter_context(nc.sbuf_tensor("bwr", [1, E], BF16))
        s_sb = _stk.enter_context(nc.sbuf_tensor("s_sb", [128, E], F32))
        bb = _stk.enter_context(nc.sbuf_tensor("bb", [128, D], F32))
        projb = _stk.enter_context(nc.sbuf_tensor("projb", [128, H // 2, PF], BF16))
        rs_all = _stk.enter_context(nc.sbuf_tensor("rs_all", [128, NCH], F32))
        mrs_all = _stk.enter_context(nc.sbuf_tensor("mrs_all", [128, NCH], F32))
        attn_nat = _stk.enter_context(nc.sbuf_tensor("attn_nat", [128, NCH, D], BF16))
        t8b = _stk.enter_context(nc.sbuf_tensor("t8b", [128, E], BF16))
        b_sb = _stk.enter_context(nc.sbuf_tensor("b_sb", [128, E], F32))
        m16 = _stk.enter_context(nc.sbuf_tensor("m16", [16, TG // 16], I16))
        g16 = _stk.enter_context(nc.sbuf_tensor("g16", [16, TG // 16], I16))
        xidx = _stk.enter_context(nc.sbuf_tensor("xidx", [128, CAP // 16], I16))
        cidx = _stk.enter_context(nc.sbuf_tensor("cidx", [128, T // 16], I16))
        nfound = _stk.enter_context(nc.sbuf_tensor("nfound", [1, 1], U32))
        scalec = _stk.enter_context(nc.sbuf_tensor("scalec", [128, 16], F32))
        epsA = _stk.enter_context(nc.sbuf_tensor("epsA", [128, 1], F32))
        s_ldw = _stk.enter_context(nc.semaphore("s_ldw"))
        s_st = _stk.enter_context(nc.semaphore("s_st"))
        s_ldg = _stk.enter_context(nc.semaphore("s_ldg"))
        s_dg = _stk.enter_context(nc.semaphore("s_dg"))
        s_te = _stk.enter_context(nc.semaphore("s_te"))
        s_ve = _stk.enter_context(nc.semaphore("s_ve"))
        s_se = _stk.enter_context(nc.semaphore("s_se"))
        s_cc = _stk.enter_context(nc.semaphore("s_cc"))
        s_out = _stk.enter_context(nc.semaphore("s_out"))
        s_at = _stk.enter_context(nc.semaphore("s_at"))
        lg_ps = _stk.enter_context(nc.psum_tensor("lg_ps", [128, NCH, E], F32))
        block = _stk.enter_context(nc.Block())
        n = dict(ldw=0, st=0, ldg=0, dg=0, te=0, ve=0, se=0, cc=0, out=0)

        @block.sync
        def _(sync):
            for dst, src in ((ident, ident_in), (iota8, iota8_in),
                             (giota1, giota1_in), (giotat, giotat_in),
                             (onesr, ones_r),
                             (onesc, ones_c), (myexp, myexp_in),
                             (coreid, coreid_in), (boutr, bout_in)):
                sync.dma_start(out=dst[:], in_=src).then_inc(s_ldw, 16)
            sync.dma_start(out=wrf[:],
                           in_=wr_in.rearrange("(k p) e -> p k e", p=128)
                           ).then_inc(s_ldw, 16)
        n["ldw"] += 10 * 16
        P0_LDW = n["ldw"]

        @block.gpsimd
        def _(gpsimd):
            gpsimd.dma_start(out=identb[:], in_=ident_in).then_inc(s_ldg, 16)
            gpsimd.dma_start(out=sltb[:], in_=slt_in).then_inc(s_ldg, 16)
            gpsimd.dma_start(out=onesrb[:], in_=ones_r).then_inc(s_ldg, 16)
            gpsimd.dma_start(out=wrb[:],
                             in_=wr_in.rearrange("(k p) e -> p k e", p=128)
                             ).then_inc(s_ldg, 16)
            for h in range(H):
                gpsimd.dma_start(out=projb[64 * (h % 2):64 * (h % 2) + 64, h // 2, :],
                                 in_=proj_in[h]).then_inc(s_ldg, 16)
        n["ldg"] += (4 + H) * 16
        P0_LDG = n["ldg"]

        with (
            nc.sbuf_tensor("woutf", [128, KD, D], F32) as woutf,
            nc.sbuf_tensor("boutk", [128, KD], F32) as boutk,
            nc.sbuf_tensor("srow", [1, E], F32) as srow,
            nc.sbuf_tensor("bwrf", [1, E], F32) as bwrf,
            nc.psum_tensor("pp_a", [128, 512], F32) as pp_a,
            nc.psum_tensor("pp_b", [128, E], F32) as pp_b,
            nc.psum_tensor("pp_c", [1, E], F32) as pp_c,
        ):
            @block.sync
            def _(sync):
                sync.dma_start(out=woutf[:],
                               in_=wout_n.rearrange("(k p) d -> p k d", p=128)
                               ).then_inc(s_ldw, 16)
                sync.dma_start(out=boutk[:],
                               in_=bout_in.rearrange("o (k p) -> p (o k)", p=128),
                               allow_slow_non_contiguous=True
                               ).then_inc(s_ldw, 16)
            n["ldw"] += 2 * 16
            P0B_LDW = n["ldw"]

            te0, ve0 = n["te"], n["ve"]

            @block.tensor
            def _(tensor):
                tensor.wait_ge(s_ldw, P0B_LDW)
                for dk in range(KD):
                    if dk >= 1:
                        tensor.wait_ge(s_ve, ve0 + dk)
                    mm = None
                    for k in range(KD):
                        mm = tensor.matmul(pp_b[:],
                                           woutf[:, k, 128 * dk:128 * dk + 128],
                                           wrf[:, k, :], start=(k == 0),
                                           stop=(k == KD - 1))
                    mm.then_inc(s_te, 1)
                mm = None
                for k in range(KD):
                    mm = tensor.matmul(pp_c[:], onesc[:], wrf[:, k, :],
                                       start=(k == 0), stop=(k == KD - 1))
                mm.then_inc(s_te, 1)
                tensor.wait_ge(s_ve, ve0 + KD + 1)
                mm = None
                for k in range(KD):
                    mm = tensor.matmul(pp_c[:], boutk[:, k:k + 1], wrf[:, k, :],
                                       start=(k == 0), stop=(k == KD - 1))
                mm.then_inc(s_te, 1)
                tensor.matmul(pp_a[:], onesr[:], boutr[:, 0:512], start=True,
                              stop=True)
                tensor.matmul(pp_b[:], onesr[:], srow[:], start=True,
                              stop=True).then_inc(s_te, 1)
                tensor.wait_ge(s_ve, ve0 + KD + 4)
                tensor.matmul(pp_a[:], onesr[:], boutr[:, 512:1024], start=True,
                              stop=True).then_inc(s_te, 1)
            n["te"] += KD + 4

            @block.vector
            def _(vector):
                vector.wait_ge(s_ldw, P0_LDW)
                vector.tensor_scalar_mul(epsA[:], onesc[:], 1e-5)
                for dk in range(KD):
                    vector.wait_ge(s_te, te0 + dk + 1)
                    vector.tensor_copy(m_sb[:, dk, :], pp_b[:]).then_inc(s_ve, 1)
                vector.wait_ge(s_te, te0 + KD + 1)
                vector.tensor_copy(srow[:], pp_c[:]).then_inc(s_ve, 1)
                vector.wait_ge(s_te, te0 + KD + 2)
                vector.tensor_copy(bwrf[:], pp_c[:]).then_inc(s_ve, 1)
                vector.wait_ge(s_te, te0 + KD + 3)
                vector.tensor_copy(bb[:, 0:512], pp_a[:]).then_inc(s_ve, 1)
                vector.tensor_copy(s_sb[:], pp_b[:]).then_inc(s_ve, 1)
                vector.wait_ge(s_te, te0 + KD + 4)
                vector.tensor_copy(bb[:, 512:1024], pp_a[:]).then_inc(s_ve, 1)
            n["ve"] += KD + 5

            @block.scalar
            def _(scalar):
                scalar.wait_ge(s_ve, ve0 + KD + 2)
                scalar.copy(bwr[:], bwrf[:]).then_inc(s_se, 1)
            n["se"] += 1
        P0_TE, P0_VE = n["te"], n["ve"]

        with (
            nc.sbuf_tensor("qT", [128, KD, T], BF16) as qT,
            nc.sbuf_tensor("kT", [128, KD, T], BF16) as kT,
            nc.sbuf_tensor("v_ext", [128, NCH, H, HD + 1], BF16) as v_ext,
        ):
            with (
                nc.sbuf_tensor("xTb", [128, KD, T], BF16) as xTb,
                nc.sbuf_tensor("wqp", [128, KD, D], BF16) as wqp,
                nc.psum_tensor("qk_ps", [128, 2, 512], F32) as qk_ps,
            ):
                @block.gpsimd
                def _(gpsimd):
                    gpsimd.wait_ge(s_te, P0_TE)
                    for k in range(KD):
                        gpsimd.dma_start(out=xTb[:, k, :],
                                         in_=xT_in[128 * k:128 * k + 128, :]
                                         ).then_inc(s_ldg, 16)
                n["ldg"] += KD * 16
                P1_XT = n["ldg"]

                def emit_part(part, kind):
                    te0, ve0 = n["te"], n["ve"]
                    te_before = n["te"]

                    @block.gpsimd
                    def _(gpsimd):
                        if part > 0:
                            gpsimd.wait_ge(s_te, te_before)
                        for k in range(KD):
                            gpsimd.dma_start(
                                out=wqp[:, k, :],
                                in_=wqkvT[128 * k:128 * k + 128,
                                          part * D:(part + 1) * D]
                            ).then_inc(s_ldg, 16)
                    n["ldg"] += KD * 16
                    wready = n["ldg"]

                    if kind == "v":
                        @block.vector
                        def _(vector):
                            vector.memset(v_ext[:, :, :, HD:HD + 1], 1.0
                                          ).then_inc(s_ve, 1)
                        n["ve"] += 1
                        ve0 = n["ve"]

                    @block.tensor
                    def _(tensor):
                        if part == 0:
                            tensor.wait_ge(s_ve, P0_VE)
                        tensor.wait_ge(s_ldg, max(wready, P1_XT))
                        i = 0
                        for a in range(KD if kind != "v" else NCH):
                            for b_ in range(4 if kind != "v" else 2):
                                if i >= 2:
                                    tensor.wait_ge(s_ve, ve0 + i - 1)
                                ps = qk_ps[:, i % 2, :]
                                mm = None
                                for k in range(KD):
                                    if kind != "v":
                                        mm = tensor.matmul(
                                            ps, wqp[:, k, 128 * a:128 * a + 128],
                                            xTb[:, k, 512 * b_:512 * b_ + 512],
                                            start=(k == 0), stop=(k == KD - 1))
                                    else:
                                        mm = tensor.matmul(
                                            ps, xTb[:, k, 128 * a:128 * a + 128],
                                            wqp[:, k, 512 * b_:512 * b_ + 512],
                                            start=(k == 0), stop=(k == KD - 1))
                                mm.then_inc(s_te, 1)
                                i += 1

                    @block.vector
                    def _(vector):
                        i = 0
                        for a in range(KD if kind != "v" else NCH):
                            for b_ in range(4 if kind != "v" else 2):
                                vector.wait_ge(s_te, te0 + i + 1)
                                if kind == "q":
                                    vector.tensor_copy(
                                        qT[:, a, 512 * b_:512 * b_ + 512],
                                        qk_ps[:, i % 2, :]).then_inc(s_ve, 1)
                                elif kind == "k":
                                    vector.tensor_copy(
                                        kT[:, a, 512 * b_:512 * b_ + 512],
                                        qk_ps[:, i % 2, :]).then_inc(s_ve, 1)
                                else:
                                    vector.tensor_copy(
                                        v_ext[:, a, 8 * b_:8 * b_ + 8, 0:HD],
                                        qk_ps[:, i % 2, :].rearrange(
                                            "p (h f) -> p h f", h=8)
                                    ).then_inc(s_ve, 1)
                                i += 1
                    n["te"] += 32
                    n["ve"] += 32

                emit_part(0, "q")
                emit_part(1, "k")
                emit_part(2, "v")

                @block.tensor
                def _(tensor):
                    mm = None
                    for j in range(NCH):
                        for k in range(KD):
                            mm = tensor.matmul(lg_ps[:, j, :],
                                               xTb[:, k, 128 * j:128 * j + 128],
                                               wrb[:, k, :],
                                               start=(j == 0 and k == 0),
                                               stop=False)
                    mm.then_inc(s_te, 1)
                n["te"] += 1
            P1_TE, P1_VE = n["te"], n["ve"]

            if upto <= 1:
                _nve = n["ve"]; _nldg = n["ldg"]; _ndg = n["dg"]; _nte = n["te"]
                @block.scalar
                def _(scalar):
                    scalar.wait_ge(s_ve, _nve)
                    scalar.wait_ge(s_te, _nte)
                    scalar.dma_start(out=out_h[0:128, :], in_=bb[:]).then_inc(s_out, 16)
                    scalar.wait_ge(s_out, 16)
                @block.gpsimd
                def _(gpsimd):
                    gpsimd.wait_ge(s_ldg, _nldg)
                    gpsimd.wait_ge(s_dg, _ndg)
                mybir.codegen_inst_isa_subclasses(nc)
                return nc

            with (
                nc.sbuf_tensor("kp_h", [128, NCH, PF], BF16) as kp_h,
                nc.sbuf_tensor("qp_h", [128, 4, 512], BF16) as qp_h,
                nc.sbuf_tensor("kv_h", [128, HD + 1], BF16) as kv_h,
                nc.sbuf_tensor("el_u", [128, 2, 512], F32) as el_u,
                nc.sbuf_tensor("el_e", [128, 2, 512], F32) as el_e,
                nc.sbuf_tensor("araw", [128, 8, HD], F32) as araw,
                nc.sbuf_tensor("zb8b", [128, 8], F32) as zb8b,
                nc.sbuf_tensor("zb8c", [128, 8], F32) as zb8c,
                nc.sbuf_tensor("zb8d", [128, 8], F32) as zb8d,
                nc.sbuf_tensor("zb8e", [128, 8], F32) as zb8e,
                nc.psum_tensor("kp_ps", [128, 2, 4, 128], F32) as kp_ps,
                nc.psum_tensor("qp_ps", [128, 2, 512], F32) as qp_ps,
                nc.psum_tensor("pz", [128, 512], F32) as pz,
                nc.psum_tensor("at_ps", [128, 2, 8, HD], F32) as at_ps,
            ):
                def emit_attn(tensor, hprev, vb, sb):
                    for hf in range(2):
                        if hprev >= 1:
                            tensor.wait_ge(s_se, sb + 9 + 3 * hf)
                            tensor.wait_ge(s_ve, vb + 18 + 3 * hf)
                        else:
                            tensor.wait_ge(s_ve, vb + 17)
                        mm = None
                        for jj in range(8):
                            j = 8 * hf + jj
                            mm = tensor.matmul(
                                at_ps[:, hf, jj, :],
                                qp_h[:, j // 4, 128 * (j % 4):128 * (j % 4) + 128],
                                kv_h[:, 0:HD], start=True, stop=True)
                        for jj in range(8):
                            j = 8 * hf + jj
                            mm = tensor.matmul(
                                pz[:, 128 + 8 * hf + jj:129 + 8 * hf + jj],
                                qp_h[:, j // 4, 128 * (j % 4):128 * (j % 4) + 128],
                                kv_h[:, HD:HD + 1], start=True, stop=True)
                        mm.then_inc(s_at, 1)

                def emit_tail_s(scalar, hprev, ve0, se0, af0):
                    for hf in range(2):
                        scalar.wait_ge(s_at, 2 * hprev + 1 + hf)
                        scalar.wait_ge(s_ve, ve0 + 3 if hf == 1 else af0)
                        scalar.copy(araw[:], at_ps[:, hf, :, :]
                                    ).then_inc(s_se, 1)
                        scalar.wait_ge(s_ve, ve0 + 1 + 3 * hf)
                        scalar.copy(zb8c[:], zb8b[:]).then_inc(s_se, 1)
                        scalar.wait_ge(s_ve, ve0 + 2 + 3 * hf)
                        scalar.copy(zb8e[:], zb8d[:]).then_inc(s_se, 1)

                def emit_tail_v(vector, hprev, se0, sbase):
                    for hf in range(2):
                        vector.wait_ge(s_at, 2 * hprev + 1 + hf)
                        if hf == 0:
                            vector.wait_ge(s_se, max(0, sbase - 1))
                        vector.tensor_scalar_add(
                            zb8b[:], pz[:, 128 + 8 * hf:136 + 8 * hf], 1e-8
                            ).then_inc(s_ve, 1)
                        vector.wait_ge(s_se, se0 + 2 + 3 * hf)
                        vector.reciprocal(zb8d[:], zb8c[:]).then_inc(s_ve, 1)
                        vector.wait_ge(s_se, se0 + 3 + 3 * hf)
                        vector.tensor_tensor(
                            attn_nat[:, 8 * hf:8 * hf + 8,
                                     HD * hprev:HD * hprev + HD],
                            araw[:, :, :],
                            zb8e[:].unsqueeze(2).broadcast_to([128, 8, HD]),
                            ALU.mult).then_inc(s_ve, 1)

                VB, SB = [], []
                for h in range(H):
                    pb = 64 * (h % 2)
                    cb = h // 2
                    te0, ve0, se0 = n["te"], n["ve"], n["se"]
                    VB.append(ve0)
                    SB.append(se0)

                    @block.tensor
                    def _(tensor, h=h, pb=pb, cb=cb, te0=te0, ve0=ve0, se0=se0):
                        if h == 0:
                            tensor.wait_ge(s_ldg, P0_LDG)
                            tensor.wait_ge(s_ve, P1_VE)
                        for g in range(4):
                            if g >= 2:
                                tensor.wait_ge(s_ve, ve0 + 2 * g - 2)
                            elif h >= 1:
                                tensor.wait_ge(s_ve, VB[h - 1] + 6 + 2 * g)
                            mm = None
                            for jj in range(4):
                                j = 4 * g + jj
                                mm = tensor.matmul(
                                    kp_ps[:, g % 2, jj, :],
                                    kT[pb:pb + 64, cb, 128 * j:128 * j + 128],
                                    projb[pb:pb + 64, cb, :],
                                    start=True, stop=True)
                            mm.then_inc(s_te, 1)
                        if h >= 1:
                            emit_attn(tensor, h - 1, VB[h - 1], SB[h - 1])
                        for tc in range(4):
                            if tc >= 2:
                                tensor.wait_ge(s_ve, ve0 + 6 + 2 * tc)
                            tensor.matmul(qp_ps[:, tc % 2, :],
                                          projb[pb:pb + 64, cb, :],
                                          qT[pb:pb + 64, cb, 512 * tc:512 * tc + 512],
                                          start=True, stop=True).then_inc(s_te, 1)
                        tensor.wait_ge(s_ve, ve0 + 8)
                        mm = None
                        for j in range(NCH):
                            mm = tensor.matmul(pz[:, 0:HD + 1], kp_h[:, j, :],
                                               v_ext[:, j, h, :],
                                               start=(j == 0), stop=(j == NCH - 1))
                        mm.then_inc(s_te, 1)

                    @block.scalar
                    def _(scalar, h=h, ve0=ve0, se0=se0):
                        for g in range(4):
                            scalar.wait_ge(s_ve, ve0 + 2 * g + 1)
                            scalar.activation(el_e[:, g % 2, :], el_u[:, g % 2, :],
                                              ACT.Exp, bias=0.0, scale=1.0
                                              ).then_inc(s_se, 1)
                        for tc in range(4):
                            scalar.wait_ge(s_ve, ve0 + 9 + 2 * tc)
                            scalar.activation(el_e[:, tc % 2, :], el_u[:, tc % 2, :],
                                              ACT.Exp, bias=0.0, scale=1.0
                                              ).then_inc(s_se, 1)
                        if h >= 1:
                            emit_tail_s(scalar, h - 1, ve0 + 17, se0 + 8, ve0)

                    @block.vector
                    def _(vector, h=h, te0=te0, ve0=ve0, se0=se0):
                        for g in range(4):
                            vector.wait_ge(s_te, te0 + g + 1)
                            if g >= 2:
                                vector.wait_ge(s_se, se0 + g - 1)
                            vector.tensor_scalar_min(
                                el_u[:, g % 2, :],
                                kp_ps[:, g % 2, :, :].rearrange(
                                    "p a b -> p (a b)"), 0.0
                                ).then_inc(s_ve, 1)
                            vector.wait_ge(s_se, se0 + g + 1)
                            vector.scalar_tensor_tensor(
                                kp_h.rearrange("p a b -> p (a b)")[
                                    :, 512 * g:512 * g + 512],
                                kp_ps[:, g % 2, :, :].rearrange(
                                    "p a b -> p (a b)"), 0.0,
                                el_e[:, g % 2, :],
                                op0=ALU.max, op1=ALU.add).then_inc(s_ve, 1)
                        for tc in range(4):
                            vector.wait_ge(s_te, te0 + 5 + tc)
                            vector.wait_ge(s_se, se0 + 3 + tc)
                            vector.tensor_scalar_min(el_u[:, tc % 2, :],
                                                     qp_ps[:, tc % 2, :],
                                                     0.0).then_inc(s_ve, 1)
                            vector.wait_ge(s_se, se0 + 5 + tc)
                            vector.scalar_tensor_tensor(
                                qp_h[:, tc, :], qp_ps[:, tc % 2, :], 0.0,
                                el_e[:, tc % 2, :],
                                op0=ALU.max, op1=ALU.add).then_inc(s_ve, 1)
                        vector.wait_ge(s_te, te0 + 9)
                        vector.tensor_copy(kv_h[:], pz[:, 0:HD + 1]).then_inc(s_ve, 1)
                        if h >= 1:
                            emit_tail_v(vector, h - 1, se0 + 8, se0)

                    n["te"] += 9
                    n["se"] += 8 + (6 if h >= 1 else 0)
                    n["ve"] += 17 + (6 if h >= 1 else 0)

                te0, ve0, se0 = n["te"], n["ve"], n["se"]

                @block.tensor
                def _(tensor):
                    emit_attn(tensor, H - 1, VB[H - 1], SB[H - 1])

                @block.scalar
                def _(scalar):
                    emit_tail_s(scalar, H - 1, ve0, se0, ve0)

                @block.vector
                def _(vector):
                    emit_tail_v(vector, H - 1, se0, se0)

                n["se"] += 6
                n["ve"] += 6
                if upto <= 2:
                    _nve2 = n["ve"]; _nse2 = n["se"]; _dbase = n["ldg"]
                    @block.gpsimd
                    def _(gpsimd):
                        gpsimd.wait_ge(s_ve, _nve2)
                        gpsimd.wait_ge(s_se, _nse2)
                        gpsimd.dma_start(out=out_h[768:896, 0:65], in_=kv_h[:]).then_inc(s_ldg, 16)
                        gpsimd.dma_start(out=out_h[896:1024, :],
                                         in_=qp_h.rearrange("p a b -> p (a b)")[:, 0:1024]).then_inc(s_ldg, 16)
                        gpsimd.dma_start(out=out_h[1024:1152, :],
                                         in_=kp_h.rearrange("p a b -> p (a b)")[:, 0:1024]).then_inc(s_ldg, 16)
                        gpsimd.dma_start(out=out_h[1152:1280, 0:8], in_=zb8d[:]).then_inc(s_ldg, 16)
                        gpsimd.dma_start(out=out_h[1280:1408, 0:512],
                                         in_=el_e[:, 1, :]).then_inc(s_ldg, 16)
                        gpsimd.wait_ge(s_ldg, _dbase + 80)
                    n["ldg"] += 80
            P2_TE, P2_VE, P2_SE = n["te"], n["ve"], n["se"]

            if upto <= 2:
                _nve = n["ve"]; _nldg = n["ldg"]; _ndg = n["dg"]; _nte = n["te"]
                _nse = n["se"]
                @block.scalar
                def _(scalar):
                    scalar.wait_ge(s_ve, _nve)
                    scalar.wait_ge(s_te, _nte)
                    scalar.dma_start(out=out_h[0:128, :], in_=bb[:]).then_inc(s_out, 16)
                    scalar.wait_ge(s_out, 16)
                @block.gpsimd
                def _(gpsimd):
                    gpsimd.wait_ge(s_ldg, _nldg)
                    gpsimd.wait_ge(s_dg, _ndg)
                    gpsimd.wait_ge(s_ve, _nve)
                    gpsimd.wait_ge(s_se, _nse)
                    gpsimd.dma_start(out=out_h[128:256, :], in_=attn_nat[:, 0, :]).then_inc(s_ldg, 16)
                    gpsimd.dma_start(out=out_h[256:384, :], in_=attn_nat[:, 8, :]).then_inc(s_ldg, 16)
                    gpsimd.dma_start(out=out_h[384:512, :], in_=qT[:, 0, 0:1024]).then_inc(s_ldg, 16)
                    gpsimd.dma_start(out=out_h[512:640, :], in_=kT[:, 0, 0:1024]).then_inc(s_ldg, 16)
                    gpsimd.dma_start(out=out_h[640:768, :],
                                     in_=v_ext.rearrange("p a b c -> p (a b c)")[:, 0:1024]).then_inc(s_ldg, 16)
                    gpsimd.wait_ge(s_ldg, _nldg + 80)
                mybir.codegen_inst_isa_subclasses(nc)
                return nc

        with (
            nc.sbuf_tensor("w1h", [128, KD, FF // 2], BF16) as w1h,
            nc.sbuf_tensor("w2h", [128, 16, D], BF16) as w2h,
        ):
            with (
                nc.sbuf_tensor("attnT", [128, KD, T], BF16) as attnT,
                nc.sbuf_tensor("woutb", [128, KD, D], BF16) as woutb,
                nc.sbuf_tensor("xbuf", [128, 2, D], F32) as xbuf,
                nc.sbuf_tensor("v1t", [128, 2, D], F32) as v1t,
                nc.sbuf_tensor("z1t", [128, 2, D], F32) as z1t,
                nc.sbuf_tensor("z1bt", [128, 2, D], BF16) as z1bt,
                nc.sbuf_tensor("s1m", [128, 1], F32) as s1m,
                nc.sbuf_tensor("t_mu", [128, 1], F32) as t_mu,
                nc.sbuf_tensor("t_a", [128, 1], F32) as t_a,
                nc.sbuf_tensor("v_b", [128, 1], F32) as v_b,
                nc.sbuf_tensor("sdt", [128, 1], F32) as sdt,
                nc.sbuf_tensor("rsig", [128, 1], F32) as rsig,
                nc.sbuf_tensor("rsig2", [128, 1], F32) as rsig2,
                nc.sbuf_tensor("ln_scr", [128, D], F32) as ln_scr,
                nc.sbuf_tensor("lfin", [128, NCH, E], F32) as lfin,
            ):
              @block.gpsimd
              def _(gpsimd):
                  gpsimd.wait_ge(s_te, P2_TE)
                  gpsimd.wait_ge(s_at, 2 * H)
                  for k in range(KD):
                      gpsimd.dma_start(out=woutb[:, k, :],
                                       in_=woutT_in[128 * k:128 * k + 128, :]
                                       ).then_inc(s_ldg, 16)
                  for k in range(KD):
                      gpsimd.dma_start(out=w1h[:, k, :],
                                       in_=w1_in[128 * k:128 * k + 128, 0:FF // 2]
                                       ).then_inc(s_ldg, 16)
                  for m in range(16):
                      gpsimd.dma_start(out=w2h[:, m, :],
                                       in_=w2_in[128 * m:128 * m + 128, :]
                                       ).then_inc(s_ldg, 16)
              n["ldg"] += (KD + KD + 16) * 16
              P3_WOUT = n["ldg"] - (KD + 16) * 16
              P3_W12 = n["ldg"]

              with (
                  nc.psum_tensor("tr_ps", [128, 4, 1024], BF16) as tr_ps,
                  nc.psum_tensor("y1_ps", [128, 2, 512], F32) as y1_ps,
              ):
                  te0, ve0, se0 = n["te"], n["ve"], n["se"]
                  ldw0, st0 = n["ldw"], n["st"]
                  TE_C, VE_C, SE_C = 9, 17, 4

                  @block.sync
                  def _(sync):
                      sync.wait_ge(s_te, P2_TE)
                      sync.wait_ge(s_at, 2 * H)
                      sync.wait_ge(s_ve, P2_VE)
                      sync.wait_ge(s_se, P2_SE)
                      for j in range(NCH):
                          if j >= 2:
                              sync.wait_ge(s_ve, ve0 + (j - 2) * VE_C + 11)
                          sync.dma_start(out=xbuf[:, j % 2, :],
                                         in_=x_nat[128 * j:128 * j + 128, :]
                                         ).then_inc(s_ldw, 16)
                  n["ldw"] += NCH * 16

                  @block.tensor
                  def _(tensor):
                      tensor.wait_ge(s_se, P2_SE)
                      tensor.wait_ge(s_ve, P2_VE)
                      tensor.wait_ge(s_ldg, P3_WOUT)
                      for j in range(NCH):
                          for w in range(KD):
                              i = j * KD + w
                              if i >= 4:
                                  tensor.wait_ge(s_ve, ve0 + (i - 4) // KD * VE_C
                                                 + (i - 4) % KD + 1)
                              tensor.transpose(tr_ps[:, i % 4, 0:128],
                                               attn_nat[:, j, 128 * w:128 * w + 128],
                                               identb[:]).then_inc(s_te, 1)
                          tensor.wait_ge(s_ve, ve0 + j * VE_C + KD)
                          if j >= 1:
                              tensor.wait_ge(s_ve, ve0 + (j - 1) * VE_C + KD + 2)
                          mm = None
                          for dh in range(2):
                              for k in range(KD):
                                  mm = tensor.matmul(
                                      y1_ps[:, dh, :],
                                      attnT[:, k, 128 * j:128 * j + 128],
                                      woutb[:, k, 512 * dh:512 * dh + 512],
                                      start=(k == 0), stop=(k == KD - 1))
                          for k in range(KD):
                              mm = tensor.matmul(lg_ps[:, j, :],
                                                 attnT[:, k, 128 * j:128 * j + 128],
                                                 m_sb[:, k, :], start=False,
                                                 stop=False)
                          mm = tensor.matmul(lg_ps[:, j, :], onesrb[:], bwr[:],
                                             start=False, stop=(j == NCH - 1))
                          mm.then_inc(s_te, 1)

                  @block.vector
                  def _(vector):
                      for j in range(NCH):
                          for w in range(KD):
                              i = j * KD + w
                              vector.wait_ge(s_te, te0 + j * (KD + 1) + w + 1)
                              vector.tensor_copy(attnT[:, w, 128 * j:128 * j + 128],
                                                 tr_ps[:, i % 4, 0:128]).then_inc(s_ve, 1)
                          vector.wait_ge(s_te, te0 + (j + 1) * (KD + 1))
                          vector.scalar_tensor_tensor(
                              v1t[:, j % 2, 0:512], y1_ps[:, 0, :], 1.0, bb[:, 0:512],
                              op0=ALU.mult, op1=ALU.add).then_inc(s_ve, 1)
                          vector.scalar_tensor_tensor(
                              v1t[:, j % 2, 512:1024], y1_ps[:, 1, :], 1.0,
                              bb[:, 512:1024], op0=ALU.mult, op1=ALU.add
                              ).then_inc(s_ve, 1)
                          vector.wait_ge(s_ldw, ldw0 + (j + 1) * 16)
                          vector.tensor_add(v1t[:, j % 2, :], v1t[:, j % 2, :],
                                            xbuf[:, j % 2, :]).then_inc(s_ve, 1)
                          vector.tensor_reduce(s1m[:].unsqueeze(2),
                                               v1t[:, j % 2, :].unsqueeze(1),
                                               axis=AX.X, op=ALU.add
                                               ).then_inc(s_ve, 1)
                          vector.wait_ge(s_se, se0 + j * SE_C + 2)
                          vector.scalar_tensor_tensor(v_b[:], t_mu[:], t_mu[:],
                                                      t_a[:], op0=ALU.mult,
                                                      op1=ALU.subtract
                                                      ).then_inc(s_ve, 1)
                          vector.wait_ge(s_se, se0 + j * SE_C + 3)
                          vector.reciprocal(rsig[:], sdt[:]).then_inc(s_ve, 1)
                          vector.wait_ge(s_se, se0 + j * SE_C + 4)
                          if j >= 2:
                              vector.wait_ge(s_st, st0 + (j - 1) * 32)
                          vector.tensor_scalar(z1t[:, j % 2, :], v1t[:, j % 2, :],
                                               t_mu[:], rsig2[:],
                                               op0=ALU.subtract, op1=ALU.mult
                                               ).then_inc(s_ve, 1)
                          vector.tensor_copy(z1bt[:, j % 2, :], z1t[:, j % 2, :]
                                             ).then_inc(s_ve, 1)
                          vector.tensor_copy(rs_all[:, j:j + 1], rsig2[:])
                          vector.tensor_mul(mrs_all[:, j:j + 1], t_mu[:], rsig2[:]
                                            ).then_inc(s_ve, 1)
                      vector.tensor_copy(xbuf[:, 0, :], v1t[:, 0, :])
                      vector.wait_ge(s_te, te0 + NCH * (KD + 1))
                      vector.tensor_tensor(
                          lfin[:], lg_ps[:],
                          rs_all[:].unsqueeze(2).broadcast_to([128, NCH, E]),
                          ALU.mult).then_inc(s_ve, 1)
                      vector.tensor_tensor(
                          v1t[:, 0, 0:128].rearrange("p (j e) -> p j e", e=E),
                          mrs_all[:].unsqueeze(2).broadcast_to([128, NCH, E]),
                          s_sb[:].unsqueeze(1).broadcast_to([128, NCH, E]),
                          ALU.mult).then_inc(s_ve, 1)
                      vector.tensor_sub(lfin[:],
                                        lfin[:],
                                        v1t[:, 0, 0:128].rearrange("p (j e) -> p j e", e=E)
                                        ).then_inc(s_ve, 1)
                      vector.tensor_copy(lfin_dbg[:], lfin[:]).then_inc(s_ve, 1)
                      vector.tensor_copy(rs_dbg[:], rs_all[:]).then_inc(s_ve, 1)
                      vector.tensor_copy(v1_dbg[:], v1t[:, (NCH - 1) % 2, :]).then_inc(s_ve, 1)
                  n["ve"] += NCH * VE_C + 6
                  P3_VE = n["ve"]

                  @block.scalar
                  def _(scalar):
                      for j in range(NCH):
                          scalar.wait_ge(s_ve, ve0 + j * VE_C + 11)
                          scalar.activation(ln_scr[:], v1t[:, j % 2, :],
                                            ACT.Square, bias=0.0,
                                            scale=1.0 / 32.0, accum_out=t_a[:]
                                            ).then_inc(s_se, 1)
                          scalar.wait_ge(s_ve, ve0 + j * VE_C + 12)
                          scalar.mul(t_mu[:], s1m[:], 1.0 / 1024.0
                                     ).then_inc(s_se, 1)
                          scalar.wait_ge(s_ve, ve0 + j * VE_C + 13)
                          scalar.activation(sdt[:], v_b[:], ACT.Sqrt,
                                            bias=epsA[:], scale=-1.0
                                            ).then_inc(s_se, 1)
                          scalar.wait_ge(s_ve, ve0 + j * VE_C + 14)
                          scalar.copy(rsig2[:], rsig[:]).then_inc(s_se, 1)
                          scalar.wait_ge(s_ve, ve0 + j * VE_C + 16)
                          scalar.dma_start(out=z1hbm[128 * j:128 * j + 128, :],
                                           in_=z1t[:, j % 2, :]).then_inc(s_st, 16)
                          scalar.dma_start(out=x_src[128 * j:128 * j + 128, :],
                                           in_=z1bt[:, j % 2, :]).then_inc(s_st, 16)
                      scalar.wait_ge(s_ve, P3_VE)
                      scalar.dma_start(out=l_src.rearrange("(j p) e -> p j e", p=128),
                                       in_=lfin[:]).then_inc(s_st, 16)
                  n["se"] += NCH * SE_C
                  n["st"] += NCH * 2 * 16 + 16
                  n["te"] += NCH * (KD + 1)
                  P3_ST = n["st"]
                  P3_TE = n["te"]

            @block.gpsimd
            def _(gpsimd):
                for q in range(3):
                    gpsimd.wait_ge(s_st, P3_ST - 16 - (12 - 4 * q) * 32)
                    gpsimd.collective_compute(
                        "AllGather", ALU.bypass, replica_groups=RG,
                        ins=[x_src[512 * q:512 * q + 512, :]],
                        outs=[x_ag[4096 * q:4096 * q + 4096, :]]
                    ).then_inc(s_cc, 1)
                gpsimd.wait_ge(s_st, P3_ST)
                gpsimd.collective_compute("AllGather", ALU.bypass,
                                          replica_groups=RG, ins=[l_src],
                                          outs=[l_ag]).then_inc(s_cc, 1)
                gpsimd.collective_compute(
                    "AllGather", ALU.bypass, replica_groups=RG,
                    ins=[x_src[1536:2048, :]],
                    outs=[x_ag[12288:16384, :]]).then_inc(s_cc, 1)
            n["cc"] += 5

            if upto <= 3:
                _nve = n["ve"]; _nldg = n["ldg"]; _ndg = n["dg"]; _nte = n["te"]
                @block.scalar
                def _(scalar):
                    scalar.wait_ge(s_ve, _nve)
                    scalar.wait_ge(s_te, _nte)
                    scalar.dma_start(out=out_h[0:128, :], in_=bb[:]).then_inc(s_out, 16)
                    scalar.dma_start(out=out_h[128:256, :], in_=v1_dbg[:]).then_inc(s_out, 16)
                    scalar.dma_start(out=out_h[256:384, :], in_=z1hbm[1920:2048, :]).then_inc(s_out, 16)
                    scalar.dma_start(out=out_h[384:512, 0:128],
                                     in_=lfin_dbg.rearrange("p a b -> p (a b)")).then_inc(s_out, 16)
                    scalar.dma_start(out=out_h[512:640, 0:16], in_=rs_dbg[:]).then_inc(s_out, 16)
                    scalar.wait_ge(s_out, 80)
                @block.gpsimd
                def _(gpsimd):
                    gpsimd.wait_ge(s_ldg, _nldg)
                    gpsimd.wait_ge(s_dg, _ndg)
                    gpsimd.wait_ge(s_ve, _nve)
                    gpsimd.dma_start(out=out_h[640:768, :], in_=x_src[0:128, :]).then_inc(s_ldg, 16)
                    gpsimd.wait_ge(s_ldg, _nldg + 16)
                mybir.codegen_inst_isa_subclasses(nc)
                return nc

            with (
                nc.sbuf_tensor("Lg", [128, 128, E], F32) as Lg,
                nc.sbuf_tensor("r_a", [128, 128], F32) as r_a,
                nc.sbuf_tensor("r_b", [128, 128], F32) as r_b,
                nc.sbuf_tensor("r_3a", [128, E, 128], F32) as r_3a,
                nc.sbuf_tensor("r_3b", [128, E, 128], F32) as r_3b,
                nc.sbuf_tensor("eidx", [128, 128], F32) as eidx,
                nc.sbuf_tensor("ptok", [128, 128], F32) as ptok,
                nc.sbuf_tensor("keepf", [128, 128], F32) as keepf,
                nc.sbuf_tensor("gidxf", [128, 128], F32) as gidxf,
                nc.sbuf_tensor("scaleg", [128, 128], F32) as scaleg,
                nc.sbuf_tensor("scidx", [128, 128], I16) as scidx,
                nc.sbuf_tensor("val_t", [128, 16, 64], F32) as val_t,
                nc.sbuf_tensor("ztile", [128, 1056], F32) as ztile,
                nc.sbuf_tensor("xg", [128, 16, 64], F32) as xg,
                nc.sbuf_tensor("xcol", [128, 16], F32) as xcol,
                nc.psum_tensor("p4_a", [128, 2, 512], F32) as p4_a,
                nc.psum_tensor("p4_b", [128, E], F32) as p4_b,
            ):
                @block.sync
                def _(sync):
                    sync.wait_ge(s_cc, n["cc"] - 1)
                    sync.dma_start(out=Lg[:],
                                   in_=l_ag.rearrange("(p j) e -> p j e", p=128)
                                   ).then_inc(s_ldw, 16)
                n["ldw"] += 16
                P4_LG = n["ldw"]

                ve0, se0, te0 = n["ve"], n["se"], n["te"]

                @block.vector
                def _(vector):
                    vector.memset(xidx[:], 0)
                    vector.memset(cidx[:], 0)
                    vector.memset(scidx[:], 0)
                    vector.memset(val_t[:], 0.0)
                    vector.memset(ztile[:], 0.0)
                    vector.wait_ge(s_ldw, P4_LG)
                    vector.reduce_max(r_a[:].unsqueeze(2), Lg[:], axis=AX.X
                                      ).then_inc(s_ve, 1)
                    vector.tensor_tensor(
                        r_3a[:], Lg.transpose([0, 2, 1]),
                        r_a[:].unsqueeze(1).broadcast_to([128, E, 128]),
                        ALU.subtract).then_inc(s_ve, 1)
                    vector.wait_ge(s_se, se0 + 1)
                    vector.tensor_reduce(r_b[:].unsqueeze(2),
                                         r_3b.transpose([0, 2, 1]),
                                         axis=AX.X, op=ALU.add).then_inc(s_ve, 1)
                    vector.reciprocal(ptok[:], r_b[:]).then_inc(s_ve, 1)
                    vector.tensor_tensor(
                        r_3a[:], Lg.transpose([0, 2, 1]),
                        r_a[:].unsqueeze(1).broadcast_to([128, E, 128]),
                        ALU.is_equal).then_inc(s_ve, 1)
                    vector.scalar_tensor_tensor(
                        r_3a[:], r_3a[:], -999.0,
                        iota8[:].unsqueeze(2).broadcast_to([128, E, 128]),
                        op0=ALU.mult, op1=ALU.add).then_inc(s_ve, 1)
                    vector.tensor_reduce(eidx[:].unsqueeze(2),
                                         r_3a.transpose([0, 2, 1]),
                                         axis=AX.X, op=ALU.min).then_inc(s_ve, 1)
                    vector.tensor_scalar_add(eidx[:], eidx[:], 999.0
                                             ).then_inc(s_ve, 1)
                    vector.tensor_tensor(
                        r_3a[:],
                        iota8[:].unsqueeze(2).broadcast_to([128, E, 128]),
                        eidx[:].unsqueeze(1).broadcast_to([128, E, 128]),
                        ALU.is_equal).then_inc(s_ve, 1)
                    vector.engine_nop().then_inc(s_ve, 1)
                    for e_ in range(E):
                        vector.tensor_tensor_scan(r_3b[:, e_, :], r_3a[:, e_, :],
                                                  ztile[:, 0:128], 0.0, op0=ALU.add,
                                                  op1=ALU.add).then_inc(s_ve, 1)
                    with nc.allow_low_precision(reason="counts <= 128 exact in bf16"):
                        vector.tensor_reduce(t8b[:].unsqueeze(2), r_3a[:], axis=AX.X,
                                             op=ALU.add).then_inc(s_ve, 1)
                    if upto < 3.2:
                        return
                    vector.wait_ge(s_se, se0 + 2)
                    vector.engine_nop().then_inc(s_ve, 1)
                    if upto < 3.3:
                        return
                    vector.tensor_tensor(r_3b[:], r_3b[:],
                                         b_sb[:].unsqueeze(2).broadcast_to(
                                             [128, E, 128]),
                                         ALU.add).then_inc(s_ve, 1)
                    vector.tensor_mul(r_3b[:], r_3b[:], r_3a[:]).then_inc(s_ve, 1)
                    vector.tensor_add(r_3b[:, 0:4, :], r_3b[:, 0:4, :],
                                      r_3b[:, 4:8, :]).then_inc(s_ve, 1)
                    vector.tensor_add(r_3b[:, 0:2, :], r_3b[:, 0:2, :],
                                      r_3b[:, 2:4, :]).then_inc(s_ve, 1)
                    vector.tensor_add(r_a[:], r_3b[:, 0, :], r_3b[:, 1, :]
                                      ).then_inc(s_ve, 1)
                    vector.tensor_single_scalar(keepf[:], r_a[:], 2048.0, ALU.is_le
                                                ).then_inc(s_ve, 1)
                    vector.tensor_mul(scaleg[:], ptok[:], keepf[:]).then_inc(s_ve, 1)
                    vector.tensor_scalar(r_b[:], r_a[:], -1.0, 2047.0,
                                         op0=ALU.add, op1=ALU.min).then_inc(s_ve, 1)
                    vector.tensor_single_scalar(r_3a[:, 0, :], r_b[:], 512.0,
                                                ALU.is_ge)
                    vector.tensor_single_scalar(r_3a[:, 1, :], r_b[:], 1024.0,
                                                ALU.is_ge)
                    vector.tensor_single_scalar(r_3a[:, 2, :], r_b[:], 1536.0,
                                                ALU.is_ge)
                    vector.tensor_add(r_3a[:, 0, :], r_3a[:, 0, :], r_3a[:, 1, :])
                    vector.tensor_add(r_3a[:, 0, :], r_3a[:, 0, :], r_3a[:, 2, :])
                    vector.scalar_tensor_tensor(gidxf[:], eidx[:], 512.0, r_b[:],
                                                op0=ALU.mult, op1=ALU.add)
                    vector.scalar_tensor_tensor(gidxf[:], r_3a[:, 0, :], 3584.0,
                                                gidxf[:], op0=ALU.mult,
                                                op1=ALU.add).then_inc(s_ve, 1)
                    vector.tensor_scalar(r_b[:], r_a[:], -1.0, 2048.0,
                                         op0=ALU.add, op1=ALU.min).then_inc(s_ve, 1)
                    vector.scalar_tensor_tensor(keepf[:], eidx[:], 2056.0, r_b[:],
                                                op0=ALU.mult, op1=ALU.add
                                                ).then_inc(s_ve, 1)
                    vector.scalar_tensor_tensor(
                        val_t[:, :, 0].unsqueeze(2),
                        myexp[:, 0:1].unsqueeze(2).broadcast_to([128, 16, 1]),
                        512.0, giotat[:].unsqueeze(2),
                        op0=ALU.mult, op1=ALU.add).then_inc(s_ve, 1)
                    if upto < 3.4:
                        return

                @block.scalar
                def _(scalar):
                    scalar.wait_ge(s_ve, ve0 + 2)
                    scalar.activation(r_3b[:], r_3a[:], ACT.Exp).then_inc(s_se, 1)
                    if upto >= 3.2:
                        scalar.wait_ge(s_te, te0 + 1)
                        scalar.copy(b_sb[:], p4_b[:]).then_inc(s_se, 1)
                n["se"] += 1 + (1 if upto >= 3.2 else 0)

                @block.tensor
                def _(tensor):
                    if upto < 3.2:
                        return
                    tensor.wait_ge(s_ve, ve0 + 19)
                    tensor.matmul(p4_b[:], sltb[:], t8b[:], start=True, stop=True
                                  ).then_inc(s_te, 1)
                    if upto < 3.4:
                        return
                    tensor.wait_ge(s_ve, ve0 + 32)
                    for q in range(8):
                        if q >= 2:
                            tensor.wait_ge(s_ve, ve0 + 32 + q - 1)
                        tensor.transpose(p4_a[0:16, q % 2, 0:128],
                                         keepf[:, 16 * q:16 * q + 16], ident[:]
                                         ).then_inc(s_te, 1)
                    for q in range(8):
                        tensor.wait_ge(s_ve, ve0 + 32 + 6 + q + 1)
                        tensor.transpose(p4_a[0:16, q % 2, 0:128],
                                         gidxf[:, 16 * q:16 * q + 16], ident[:]
                                         ).then_inc(s_te, 1)
                    tensor.wait_ge(s_ve, ve0 + 32 + 14 + 2)
                    tensor.transpose(p4_a[:, 0, 0:128], scaleg[:], ident[:]
                                     ).then_inc(s_te, 1)

                @block.vector
                def _(vector):
                    if upto < 3.4:
                        return
                    for q in range(8):
                        vector.wait_ge(s_te, te0 + 1 + q + 1)
                        vector.tensor_copy(
                            m16[0:16, :].rearrange("p (c e) -> p c e", e=8)[:, :, q],
                            p4_a[0:16, q % 2, 0:128]).then_inc(s_ve, 1)
                    for q in range(8):
                        vector.wait_ge(s_te, te0 + 9 + q + 1)
                        vector.tensor_copy(
                            g16[0:16, :].rearrange("p (c e) -> p c e", e=8)[:, :, q],
                            p4_a[0:16, q % 2, 0:128]).then_inc(s_ve, 1)
                    vector.wait_ge(s_te, te0 + 18)
                    if upto < 3.5:
                        return
                    creg = vector.value_load(coreid[:], min_val=0, max_val=7)
                    vector.tensor_copy(r_b[:], p4_a[:, 0, 0:128])
                    vector.tensor_copy(scalec[:],
                                       r_b[:][:, bass.ts(creg, 16)]
                                       ).then_inc(s_ve, 1)
                    vector.tensor_copy(cidx[0:16, :],
                                       g16[0:16, :][:, bass.ts(creg, 128)]
                                       ).then_inc(s_ve, 1)
                    vector.tensor_copy(scidx[0:16, :],
                                       m16[0:16, :][:, bass.ts(creg, 128)]
                                       ).then_inc(s_ve, 1)
                n["ve"] += (19 + (1 if upto >= 3.2 else 0)
                            + (12 if upto >= 3.3 else 0)
                            + (16 if upto >= 3.4 else 0)
                            + (3 if upto >= 3.5 else 0))
                P4_VE = n["ve"]
                n["te"] += ((1 if upto >= 3.2 else 0)
                            + (17 if upto >= 3.4 else 0))
                P4_TE = n["te"]

                if upto < 3.6:
                    _nve = n["ve"]; _nldg = n["ldg"]; _ndg = n["dg"]; _nte = n["te"]
                    @block.scalar
                    def _(scalar):
                        scalar.wait_ge(s_ve, _nve)
                        scalar.wait_ge(s_te, _nte)
                        scalar.dma_start(out=out_h[0:128, :], in_=bb[:]).then_inc(s_out, 16)
                        scalar.wait_ge(s_out, 16)
                    @block.gpsimd
                    def _(gpsimd):
                        gpsimd.wait_ge(s_ldg, _nldg)
                        gpsimd.wait_ge(s_dg, _ndg)
                    mybir.codegen_inst_isa_subclasses(nc)
                    return nc

                GELEMS = GRID_ROWS * 64
                CH = 128 * 1056

                @block.gpsimd
                def _(gpsimd):
                    gpsimd.load_library(_LIB_MLP)
                    gpsimd.wait_ge(s_ve, ve0 + 1)
                    off = 0
                    nz = 0
                    while off < GELEMS:
                        this = min(CH, GELEMS - off)
                        gpsimd.dma_start(
                            out=grid_fl[off:off + this].rearrange(
                                "(p f) -> p f", p=128),
                            in_=ztile[:, 0:this // 128]).then_inc(s_ldg, 16)
                        off += this
                        nz += 1
                    gpsimd.wait_ge(s_ve, P4_VE)
                    gpsimd.dma_start(out=scidx[16:32, :], in_=scidx[0:16, :]
                                     ).then_inc(s_ldg, 16)
                    gpsimd.dma_start(out=cidx[16:32, :], in_=cidx[0:16, :]
                                     ).then_inc(s_ldg, 16)
                    gpsimd.wait_ge(s_ldg, n["ldg"] + (nz + 2) * 16)
                    gpsimd.dma_scatter_add(
                        grid_fl.rearrange("(r c) -> r c", c=64),
                        val_t[:], scidx[:], num_idxs=T, num_idxs_reg=T,
                        elem_size=64).then_inc(s_dg, 16)
                    gpsimd.wait_ge(s_dg, n["dg"] + 16)
                    gpsimd.collective_compute(
                        "ReduceScatter", ALU.add, replica_groups=RG,
                        ins=[grid_fl.rearrange("(r c) -> r c", c=64)],
                        outs=[rs_out]).then_inc(s_cc, 1)
                    gpsimd.wait_ge(s_cc, n["cc"] + 1)
                    gpsimd.dma_start(
                        out=xg[:],
                        in_=rs_out[0:CAP, :].rearrange("(p r) c -> p r c", p=128)
                        ).then_inc(s_ldg, 16)
                n["ldg"] += (8 + 2 + 1) * 16
                n["dg"] += 16
                n["cc"] += 1
                P4_RB = n["ldg"]

                @block.vector
                def _(vector):
                    vector.wait_ge(s_ldg, P4_RB)
                    vector.tensor_copy(xcol[:], xg[:, :, 0]).then_inc(s_ve, 1)
                n["ve"] += 1

                @block.tensor
                def _(tensor):
                    tensor.wait_ge(s_ve, n["ve"])
                    tensor.transpose(p4_a[0:16, 0, 0:128], xcol[:], ident[:]
                                     ).then_inc(s_te, 1)
                n["te"] += 1

                @block.vector
                def _(vector):
                    vector.wait_ge(s_te, n["te"])
                    vector.tensor_copy(xidx[0:16, :], p4_a[0:16, 0, 0:128]
                                       ).then_inc(s_ve, 1)
                n["ve"] += 1
                P4_VE = n["ve"]

                @block.gpsimd
                def _(gpsimd):
                    gpsimd.wait_ge(s_ve, P4_VE)
                    gpsimd.dma_start(out=xidx[16:32, :], in_=xidx[0:16, :]
                                     ).then_inc(s_ldg, 16)
                n["ldg"] += 16
                P4_LDG = n["ldg"]
                P4_CC = n["cc"]

                if upto <= 4:
                    _nve = n["ve"]; _nldg = n["ldg"]; _ndg = n["dg"]; _nte = n["te"]
                    @block.scalar
                    def _(scalar):
                        scalar.wait_ge(s_ve, _nve)
                        scalar.wait_ge(s_te, _nte)
                        scalar.dma_start(out=out_h[0:128, :], in_=bb[:]).then_inc(s_out, 16)
                        scalar.dma_start(out=out_h[256:384, :], in_=xg.rearrange("p a b -> p (a b)")).then_inc(s_out, 16)
                        scalar.dma_start(out=out_h[384:512, 0:16], in_=xcol[:]).then_inc(s_out, 16)
                        scalar.dma_start(out=out_h[512:640, 0:16], in_=scalec[:]).then_inc(s_out, 16)
                        scalar.wait_ge(s_out, 64)
                    @block.gpsimd
                    def _(gpsimd):
                        gpsimd.wait_ge(s_ldg, _nldg)
                        gpsimd.wait_ge(s_dg, _ndg)
                        gpsimd.wait_ge(s_ve, _nve)
                        gpsimd.dma_start(out=out_h[128:256, 0:128], in_=xidx[:]).then_inc(s_ldg, 16)
                        gpsimd.dma_start(out=out_h[640:768, 0:128], in_=cidx[:]).then_inc(s_ldg, 16)
                        gpsimd.dma_start(out=out_h[768:896, 0:128], in_=gidxf[:]).then_inc(s_ldg, 16)
                        gpsimd.dma_start(out=out_h[896:1024, 0:128], in_=keepf[:]).then_inc(s_ldg, 16)
                        gpsimd.dma_start(out=out_h[1024:1152, :],
                                         in_=Lg.rearrange("p a b -> p (a b)")).then_inc(s_ldg, 16)
                        gpsimd.dma_start(out=out_h[1152:1280, 0:128],
                                         in_=lfin_dbg.rearrange("p a b -> p (a b)")).then_inc(s_ldg, 16)
                        gpsimd.dma_start(out=out_h[1280:1408, 0:16], in_=rs_dbg[:]).then_inc(s_ldg, 16)
                        gpsimd.dma_start(out=out_h[1408:1536, 0:128].rearrange("p (j e) -> p j e", e=E),
                                         in_=l_ag[0:2048, :].rearrange("(j p) e -> p j e", p=128)
                                         ).then_inc(s_ldg, 16)
                        gpsimd.wait_ge(s_ldg, _nldg + 128)
                    mybir.codegen_inst_isa_subclasses(nc)
                    return nc

            with (
                nc.sbuf_tensor("xeT", [128, KD, 512], BF16) as xeT,
                nc.sbuf_tensor("hTt", [128, 16, 512], BF16) as hTt,
                nc.sbuf_tensor("yeb", [128, 4, D], F32) as yeb,
                nc.psum_tensor("h_ps", [128, 2, 512], F32) as h_ps,
                nc.psum_tensor("ye_ps", [128, 2, 512], F32) as ye_ps,
            ):
                W_RDY = [P3_W12]
                ST_H0 = [0]
                LDGB = []
                if upto < 4.8:
                    _blocks = [(0, 0)]
                elif upto < 5:
                    _blocks = [(0, b) for b in range(4)]
                else:
                    _blocks = None
                for half in range(2):
                    if _blocks is not None and all(h != half for h, _ in _blocks):
                        continue
                    if half == 1:
                        te_snap = n["te"]
                        ST_H0[0] = n["st"]

                        @block.gpsimd
                        def _(gpsimd):
                            gpsimd.wait_ge(s_te, te_snap)
                            for k in range(KD):
                                gpsimd.dma_start(
                                    out=w1h[:, k, :],
                                    in_=w1_in[128 * k:128 * k + 128, FF // 2:FF]
                                ).then_inc(s_ldg, 16)
                            for m in range(16):
                                gpsimd.dma_start(
                                    out=w2h[:, m, :],
                                    in_=w2_in[128 * (16 + m):128 * (16 + m) + 128, :]
                                ).then_inc(s_ldg, 16)
                        n["ldg"] += (KD + 16) * 16
                        W_RDY[0] = n["ldg"]

                    for b_ in range(4):
                        if _blocks is not None and (half, b_) not in _blocks:
                            continue
                        te0, ve0, se0, dg0 = n["te"], n["ve"], n["se"], n["dg"]
                        st0, ldg0 = n["st"], n["ldg"]
                        wrdy = W_RDY[0]

                        @block.gpsimd
                        def _(gpsimd):
                            if half + b_ > 0:
                                gpsimd.wait_ge(s_te, te0)
                            else:
                                gpsimd.wait_ge(s_ldg, P4_LDG)
                                gpsimd.wait_ge(s_ve, P4_VE)
                                gpsimd.wait_ge(s_te, P4_TE)
                            gpsimd.dma_gather(xeT[:], x_ag,
                                              xidx[:, 32 * b_:32 * b_ + 32],
                                              num_idxs=512, num_idxs_reg=512,
                                              elem_size=D, transpose=True
                                              ).then_inc(s_dg, 16)
                        n["dg"] += 16

                        @block.tensor
                        def _(tensor):
                            if upto < 4.4:
                                return
                            if half + b_ == 0:
                                tensor.wait_ge(s_ve, ve0)
                            tensor.wait_ge(s_dg, dg0 + 16)
                            tensor.wait_ge(s_ldg, wrdy)
                            for m in range(16):
                                if m >= 2:
                                    tensor.wait_ge(s_se, se0 + m - 1)
                                mm = None
                                for k in range(KD):
                                    mm = tensor.matmul(
                                        h_ps[:, m % 2, :],
                                        w1h[:, k, 128 * m:128 * m + 128],
                                        xeT[:, k, :], start=(k == 0),
                                        stop=(k == KD - 1))
                                mm.then_inc(s_te, 1)
                            if upto < 4.6:
                                return
                            tensor.wait_ge(s_se, se0 + 16)
                            i = 0
                            for ss in range(4):
                                for dh in range(2):
                                    if i >= 2:
                                        tensor.wait_ge(s_ve, ve0 + i - 1)
                                    mm = None
                                    for m in range(16):
                                        mm = tensor.matmul(
                                            ye_ps[:, i % 2, :],
                                            hTt[:, m, 128 * ss:128 * ss + 128],
                                            w2h[:, m, 512 * dh:512 * dh + 512],
                                            start=(m == 0), stop=(m == 15))
                                    mm.then_inc(s_te, 1)
                                    i += 1
                        n["te"] += (16 if upto >= 4.4 else 0) + (8 if upto >= 4.6 else 0)

                        @block.scalar
                        def _(scalar):
                            if upto < 4.4:
                                return
                            for m in range(16):
                                scalar.wait_ge(s_te, te0 + m + 1)
                                scalar.activation(hTt[:, m, :], h_ps[:, m % 2, :],
                                                  ACT.Relu).then_inc(s_se, 1)
                            if upto < 4.6:
                                return
                            if half == 0:
                                for ss in range(4):
                                    scalar.wait_ge(s_ve, ve0 + 2 * ss + 2)
                                    scalar.dma_start(
                                        out=ye_src[512 * b_ + 128 * ss:
                                                   512 * b_ + 128 * ss + 128, :],
                                        in_=yeb[:, ss, :]).then_inc(s_st, 16)
                        n["se"] += 16 if upto >= 4.4 else 0
                        if half == 0 and upto >= 4.6:
                            n["st"] += 4 * 16

                        @block.vector
                        def _(vector):
                            if upto < 4.6:
                                return
                            i = 0
                            for ss in range(4):
                                for dh in range(2):
                                    vector.wait_ge(s_te, te0 + 16 + i + 1)
                                    if half == 0 and b_ > 0:
                                        vector.wait_ge(s_st, st0 - 64 + 16 * (ss + 1))
                                    elif half == 1 and b_ == 0:
                                        vector.wait_ge(s_st, ST_H0[0])
                                    elif half == 1:
                                        vector.wait_ge(s_ldg, ldg0 - 64 + 16 * (ss + 1))
                                    vector.tensor_copy(yeb[:, ss, 512 * dh:512 * dh + 512],
                                                       ye_ps[:, i % 2, :]
                                                       ).then_inc(s_ve, 1)
                                    i += 1
                        n["ve"] += 8 if upto >= 4.6 else 0

                        if half == 1:
                            @block.gpsimd
                            def _(gpsimd):
                                if b_ == 0:
                                    gpsimd.wait_ge(s_st, ST_H0[0])
                                for ss in range(4):
                                    gpsimd.wait_ge(s_ve, ve0 + 2 * ss + 2)
                                    gpsimd.dma_start(
                                        out=ye_src[512 * b_ + 128 * ss:
                                                   512 * b_ + 128 * ss + 128, :],
                                        in_=yeb[:, ss, :], accum_op=ALU.add
                                    ).then_inc(s_ldg, 16)
                            n["ldg"] += 4 * 16
                            LDGB.append(n["ldg"])
                            if upto >= 5:
                                _ldgb = n["ldg"]

                                @block.gpsimd
                                def _(gpsimd):
                                    gpsimd.wait_ge(s_ldg, _ldgb)
                                    gpsimd.collective_compute(
                                        "AllGather", ALU.bypass,
                                        replica_groups=RG,
                                        ins=[ye_src[512 * b_:512 * b_ + 512, :]],
                                        outs=[ye_ag[4096 * b_:4096 * b_ + 4096, :]]
                                    ).then_inc(s_cc, 1)
                                n["cc"] += 1
                P5_ST = n["st"]
                P5_LDG = n["ldg"]

                if upto < 5:
                    _nve = n["ve"]; _nldg = n["ldg"]; _ndg = n["dg"]
                    _nte = n["te"]; _nse = n["se"]; _nst = n["st"]
                    @block.scalar
                    def _(scalar):
                        scalar.wait_ge(s_ve, _nve)
                        scalar.wait_ge(s_te, _nte)
                        scalar.wait_ge(s_se, _nse)
                        scalar.wait_ge(s_st, _nst)
                        scalar.dma_start(out=out_h[0:128, :], in_=bb[:]).then_inc(s_out, 16)
                        scalar.wait_ge(s_out, 16)
                    @block.gpsimd
                    def _(gpsimd):
                        gpsimd.wait_ge(s_ldg, _nldg)
                        gpsimd.wait_ge(s_dg, _ndg)
                    mybir.codegen_inst_isa_subclasses(nc)
                    return nc

            P6_CC = n["cc"]

            if upto <= 5:
                _nve = n["ve"]; _nldg = n["ldg"]; _ndg = n["dg"]; _nte = n["te"]
                @block.scalar
                def _(scalar):
                    scalar.wait_ge(s_ve, _nve)
                    scalar.wait_ge(s_te, _nte)
                    scalar.dma_start(out=out_h[0:128, :], in_=bb[:]).then_inc(s_out, 16)
                    scalar.wait_ge(s_out, 16)
                @block.gpsimd
                def _(gpsimd):
                    gpsimd.wait_ge(s_ldg, _nldg)
                    gpsimd.wait_ge(s_dg, _ndg)
                mybir.codegen_inst_isa_subclasses(nc)
                return nc

            with (
                nc.sbuf_tensor("ycomb", [128, 2, 4, D], F32) as ycomb,
                nc.sbuf_tensor("zb1", [128, 2, 4, D], F32) as zb1,
                nc.sbuf_tensor("o_t", [128, 2, D], F32) as o_t,
                nc.sbuf_tensor("s1b", [128, 1], F32) as s1b,
                nc.sbuf_tensor("tmub", [128, 1], F32) as tmub,
                nc.sbuf_tensor("tab", [128, 1], F32) as tab,
                nc.sbuf_tensor("vbb", [128, 1], F32) as vbb,
                nc.sbuf_tensor("sdb", [128, 1], F32) as sdb,
                nc.sbuf_tensor("rsb", [128, 1], F32) as rsb,
                nc.sbuf_tensor("rsb2", [128, 1], F32) as rsb2,
                nc.sbuf_tensor("scr6", [128, D], F32) as scr6,
            ):
                ldw_z0 = n["ldw"]

                ve0_p6 = n["ve"]

                @block.sync
                def _(sync):
                    sync.wait_ge(s_cc, P6_CC)
                    for cb in range(4):
                        if cb >= 2:
                            sync.wait_ge(s_ve, ve0_p6 + (cb - 2) * 20 + 16)
                        sync.dma_start(
                            out=zb1[:, cb % 2, :, :],
                            in_=z1hbm[512 * cb:512 * cb + 512, :].rearrange(
                                "(u p) d -> p u d", p=128)).then_inc(s_ldw, 16)
                n["ldw"] += 4 * 16

                dg_c0 = n["dg"]

                @block.gpsimd
                def _(gpsimd):
                    gpsimd.wait_ge(s_ldg, P4_LDG)
                    gpsimd.wait_ge(s_cc, P6_CC)
                    for cb in range(4):
                        if cb >= 2:
                            gpsimd.wait_ge(s_ve, ve0_p6 + (cb - 2) * 20 + 20)
                        gpsimd.dma_gather(ycomb[:, cb % 2, :, :], ye_ag,
                                          cidx[:, 32 * cb:32 * cb + 32],
                                          num_idxs=512, num_idxs_reg=512,
                                          elem_size=D).then_inc(s_dg, 16)
                n["dg"] += 4 * 16

                ve0, se0 = n["ve"], n["se"]
                VE_U = 5

                @block.vector
                def _(vector):
                    for cb in range(4):
                        vector.wait_ge(s_dg, dg_c0 + (cb + 1) * 16)
                        vector.wait_ge(s_ldw, ldw_z0 + (cb + 1) * 16)
                        for u in range(4):
                            uu = 4 * cb + u
                            vector.scalar_tensor_tensor(
                                ycomb[:, cb % 2, u, :], ycomb[:, cb % 2, u, :],
                                scalec[:, uu:uu + 1], zb1[:, cb % 2, u, :],
                                op0=ALU.mult, op1=ALU.add).then_inc(s_ve, 1)
                            vector.tensor_reduce(
                                s1b[:].unsqueeze(2),
                                ycomb[:, cb % 2, u, :].unsqueeze(1),
                                axis=AX.X, op=ALU.add).then_inc(s_ve, 1)
                            vector.wait_ge(s_se, se0 + uu * 4 + 2)
                            vector.scalar_tensor_tensor(
                                vbb[:], tmub[:], tmub[:], tab[:],
                                op0=ALU.mult, op1=ALU.subtract
                                ).then_inc(s_ve, 1)
                            vector.wait_ge(s_se, se0 + uu * 4 + 3)
                            vector.reciprocal(rsb[:], sdb[:]).then_inc(s_ve, 1)
                            vector.wait_ge(s_se, se0 + uu * 4 + 4)
                            if uu >= 2:
                                vector.wait_ge(s_out, (uu - 1) * 16)
                            vector.tensor_scalar(o_t[:, uu % 2, :],
                                                 ycomb[:, cb % 2, u, :],
                                                 tmub[:], rsb2[:],
                                                 op0=ALU.subtract, op1=ALU.mult
                                                 ).then_inc(s_ve, 1)
                n["ve"] += 4 * 4 * VE_U

                @block.scalar
                def _(scalar):
                    for cb in range(4):
                        for u in range(4):
                            uu = 4 * cb + u
                            scalar.wait_ge(s_ve, ve0 + uu * VE_U + 1)
                            scalar.activation(scr6[:], ycomb[:, cb % 2, u, :],
                                              ACT.Square, bias=0.0,
                                              scale=1.0 / 32.0,
                                              accum_out=tab[:]).then_inc(s_se, 1)
                            scalar.wait_ge(s_ve, ve0 + uu * VE_U + 2)
                            scalar.mul(tmub[:], s1b[:], 1.0 / 1024.0
                                       ).then_inc(s_se, 1)
                            scalar.wait_ge(s_ve, ve0 + uu * VE_U + 3)
                            scalar.activation(sdb[:], vbb[:], ACT.Sqrt,
                                              bias=epsA[:], scale=-1.0
                                              ).then_inc(s_se, 1)
                            scalar.wait_ge(s_ve, ve0 + uu * VE_U + 4)
                            scalar.copy(rsb2[:], rsb[:]).then_inc(s_se, 1)
                            scalar.wait_ge(s_ve, ve0 + uu * VE_U + 5)
                            scalar.dma_start(
                                out=out_h[512 * cb + 128 * u:
                                          512 * cb + 128 * u + 128, :],
                                in_=o_t[:, uu % 2, :]).then_inc(s_out, 16)
                    scalar.wait_ge(s_out, 16 * 16)
                n["se"] += 16 * 4
                n["out"] += 16 * 16

                @block.gpsimd
                def _(gpsimd):
                    gpsimd.wait_ge(s_dg, n["dg"])

    mybir.codegen_inst_isa_subclasses(nc)
    return nc


def _consts():
    ident = np.eye(128, dtype=np.float32)
    slt = np.triu(np.ones((128, 128), np.float32), 1)
    iota8 = np.tile(np.arange(E, dtype=np.float32), (128, 1))
    giota1 = (np.arange(128, dtype=np.float32)[:, None] * 128
              + np.arange(128, dtype=np.float32)[None, :] + 1.0)
    giotat = (np.arange(128, dtype=np.float32)[:, None]
              + 128.0 * np.arange(16, dtype=np.float32)[None, :]
              + 3584.0 * (np.arange(16, dtype=np.float32)[None, :] // 4))
    ones_r = np.ones((1, 128), np.float32)
    ones_c = np.ones((128, 1), np.float32)
    return ident, slt, iota8, giota1, giotat, ones_r, ones_c


def kernel(L0_attn_x, L0_attn_W_qkv, L0_attn_proj, L0_attn_W_out, L0_attn_b_out,
           L0_ffn_Wr, L0_ffn_W1, L0_ffn_W2):
    x = np.ascontiguousarray(np.asarray(L0_attn_x, dtype=np.float32))
    wqkv = np.asarray(L0_attn_W_qkv, dtype=np.float32)
    proj = np.ascontiguousarray(np.asarray(L0_attn_proj, dtype=np.float32))
    wout = np.ascontiguousarray(np.asarray(L0_attn_W_out, dtype=np.float32))
    bout = np.ascontiguousarray(np.asarray(L0_attn_b_out, dtype=np.float32)
                                .reshape(1, D))
    wr = np.ascontiguousarray(np.asarray(L0_ffn_Wr, dtype=np.float32))
    w1 = np.asarray(L0_ffn_W1, dtype=np.float32)
    w2 = np.asarray(L0_ffn_W2, dtype=np.float32)

    wqkvT = np.ascontiguousarray(wqkv.T)
    woutT = np.ascontiguousarray(wout.T)
    ident, slt, iota8, giota1, giotat, ones_r, ones_c = _consts()

    if "nc" not in _CACHE:
        _CACHE["nc"] = build()
    nc = _CACHE["nc"]

    in_maps = []
    for c in range(NCORE):
        in_maps.append({
            "x_nat": np.ascontiguousarray(x[c]),
            "xT_in": np.ascontiguousarray(x[c].T),
            "wqkvT": wqkvT,
            "proj_in": proj,
            "woutT_in": woutT,
            "wout_n": wout,
            "bout_in": bout,
            "wr_in": wr,
            "w1_in": np.ascontiguousarray(w1[c]),
            "w2_in": np.ascontiguousarray(w2[c]),
            "ident_in": ident,
            "slt_in": slt,
            "iota8_in": iota8,
            "giota1_in": giota1,
            "giotat_in": giotat,
            "ones_r": ones_r,
            "ones_c": ones_c,
            "myexp_in": np.full((128, 1), c, np.float32),
            "coreid_in": np.array([[c]], np.uint32),
        })
    res = run_bass_kernel_spmd(nc, in_maps, core_ids=list(range(NCORE)))
    return np.stack([res.results[c]["out"] for c in range(NCORE)], axis=0)
